# version 17
# baseline (speedup 1.0000x reference)
import contextlib

import numpy as np
import ml_dtypes

import concourse.bass as bass
import concourse.mybir as mybir
from concourse.bass_utils import run_bass_kernel_spmd

F32 = mybir.dt.float32
BF16 = mybir.dt.bfloat16
AF = mybir.ActivationFunctionType
ALU = mybir.AluOpType
AX = mybir.AxisListType
PI = float(np.pi)

NCORES = 8
DM = 2048
NDT = DM // 128
HQL = 8
HKV, DH = 8, 128
FF = 8192
NA = FF // 128
EPS_LN = 1e-5
TBATCH = 2048
TOWN = 1024


def _hadamard(n):
    h = np.array([[1.0]])
    while h.shape[0] < n:
        h = np.block([[h, h], [h, -h]])
    return h


class TK:

    def __init__(self, nc, ctx, names):
        self.h = {n: ctx.enter_context(nc.semaphore(n)) for n in names}
        self.cnt = {n: 0 for n in names}
        self.waited = {}

    def inc(self, instr, name, n=1):
        self.cnt[name] += n
        instr.then_inc(self.h[name], n)
        return (name, self.cnt[name])

    def tok(self, name):
        return (name, self.cnt[name])

    def wait(self, eng, ek, *toks):
        for t in toks:
            if t is None:
                continue
            name, v = t
            if v > 0 and self.waited.get((ek, name), 0) < v:
                eng.wait_ge(self.h[name], v)
                self.waited[(ek, name)] = v


def build_program(t_own=TOWN, t_batch=TBATCH, t_chunk=512, debug=False):
    nc = bass.Bass("TRN2", target_bir_lowering=False)
    n_ac = t_batch // t_chunk
    n_oc = t_own // t_chunk
    NT = t_chunk // 128
    TH = t_chunk // 2
    NSUB = t_own // 128

    XBAT = nc.dram_tensor("XBAT", [t_batch, DM], F32, kind="ExternalInput")
    XOWN = nc.dram_tensor("XOWN", [t_own, DM], F32, kind="ExternalInput")
    OUT = nc.dram_tensor("OUT", [t_own, DM], F32, kind="ExternalOutput")
    WQTS = nc.dram_tensor("WQTS", [2, 128, NDT * 512], F32, kind="ExternalInput")
    WKT = nc.dram_tensor("WKT", [2, 128, NDT * 512], F32, kind="ExternalInput")
    WVT = nc.dram_tensor("WVT", [2, 128, NDT * 512], F32, kind="ExternalInput")
    WOT = nc.dram_tensor("WOT", [4, 128, NDT * 512], F32, kind="ExternalInput")
    W1T = nc.dram_tensor("W1T", [32, 128, NDT * 256], F32, kind="ExternalInput")
    W3T = nc.dram_tensor("W3T", [32, 128, NDT * 256], F32, kind="ExternalInput")
    W2T = nc.dram_tensor("W2T", [32, 128, 4 * 1024], F32, kind="ExternalInput")
    BQ = nc.dram_tensor("BQ", [1, HQL * DH], F32, kind="ExternalInput")
    BK = nc.dram_tensor("BK", [1, HKV * DH], F32, kind="ExternalInput")
    BV = nc.dram_tensor("BV", [1, HKV * DH], F32, kind="ExternalInput")
    BO = nc.dram_tensor("BO", [1, DM], F32, kind="ExternalInput")
    RTD = nc.dram_tensor("RT", [DH, 64], F32, kind="ExternalInput")
    IDENTD = nc.dram_tensor("IDENT", [128, 128], F32, kind="ExternalInput")
    ONESD = nc.dram_tensor("ONESR", [1, 128], F32, kind="ExternalInput")
    H128D = nc.dram_tensor("H128S", [128, 128], F32, kind="ExternalInput")
    BLKD = nc.dram_tensor("BLK64", [128, 128], F32, kind="ExternalInput")
    COSD = nc.dram_tensor("COS", [t_batch, 128], F32, kind="ExternalInput")
    SIND = nc.dram_tensor("SINS", [t_batch, 128], F32, kind="ExternalInput")
    S1 = nc.dram_tensor("S1", [FF, t_chunk], BF16)
    S2 = nc.dram_tensor("S2", [FF, t_chunk], BF16)
    W1Bd = nc.dram_tensor("W1Bd", [32, 128, NDT * 256], BF16)
    W3Bd = nc.dram_tensor("W3Bd", [32, 128, NDT * 256], BF16)
    W2Bd = nc.dram_tensor("W2Bd", [32, 128, 4 * 1024], BF16)

    dbg = {}
    if debug:
        for name, shape in [
            ("DBG_QP", [128, NT * HQL * DH]), ("DBG_KP", [128, NT * HKV * DH]),
            ("DBG_A", [128, 64]), ("DBG_MIXT", [128, HQL * t_batch]),
            ("DBG_S", [128, NSUB * DM]), ("DBG_U", [128, NA * t_chunk]),
            ("DBG_FB", [128, NA * t_chunk]), ("DBG_QROT", [128, NT * HQL * DH]),
            ("DBG_P0Q", [128, HQL * 64]),
        ]:
            dbg[name] = nc.dram_tensor(name, shape, BF16, kind="ExternalOutput")

    ctx = contextlib.ExitStack()
    sb = lambda nm, shp, dt: ctx.enter_context(nc.sbuf_tensor(nm, shp, dt))

    tk = TK(nc, ctx, [
        "dxa", "dxb", "dwa", "dwb", "dwc", "dwd", "dcst", "dh", "dh2", "dh3",
        "dh4", "dh5", "dwe", "dbg",
        "do0", "do1", "do2", "do3", "xr0", "xr1", "xr2", "xr3", "pe", "dv", "ac", "po",
    ])

    IDENT = sb("IDENTs", [128, 128], BF16)
    ONES = sb("ONESs", [1, 128], BF16)
    H128S = sb("H128Sb", [128, 128], BF16)
    BLK64 = sb("BLK64b", [128, 128], BF16)
    RT16 = sb("RT16", [DH, 64], BF16)
    COSB = sb("COSB", [128, t_batch], BF16)
    SINSB = sb("SINSB", [128, t_batch], BF16)
    BQs = sb("BQs", [1, HQL * DH], BF16)
    BKs = sb("BKs", [1, HKV * DH], BF16)
    BVs = sb("BVs", [1, HKV * DH], BF16)
    BOs = sb("BOs", [1, DM], BF16)
    XNT = sb("XNT", [128, NDT * t_chunk], BF16)
    SBf = sb("SBf", [128, NSUB * DM], BF16)
    WA = sb("WA", [128, 8192], BF16)
    WB = sb("WB", [128, 8192], BF16)
    ROWS = sb("ROWS", [128, 160], F32)

    P = [ctx.enter_context(nc.psum_tensor(f"p{i}", [128, 512], F32)) for i in range(6)]

    DVE, ACT = "dv", "ac"

    def rcol(ts, idx, n=1):
        return ROWS[:, ts * 24 + idx: ts * 24 + idx + n]

    C_SUM, C_SUMSQ, C_MEAN, C_M2, C_VAR, C_SV, C_RSTD = 0, 1, 2, 3, 4, 5, 6
    C_NQ = 8

    with nc.Block() as blk:
        def cst(g):
            for dst, src in [
                (IDENT[:, :], IDENTD.ap()), (ONES[:, :], ONESD.ap()),
                (H128S[:, :], H128D.ap()), (BLK64[:, :], BLKD.ap()),
                (RT16[:, :], RTD.ap()), (BQs[:, :], BQ.ap()),
                (BKs[:, :], BK.ap()), (BVs[:, :], BV.ap()), (BOs[:, :], BO.ap()),
            ]:
                tk.inc(g.dma_start(dst, src), "dcst", 16)
            nslot = t_batch // 128
            for dstt, srct in [(COSB, COSD), (SINSB, SIND)]:
                tk.inc(g.dma_start(
                    bass.AP(dstt, 0, [[t_batch, 128], [128, nslot], [1, 128]]),
                    bass.AP(srct, 0, [[128, 128], [128 * 128, nslot], [1, 128]]),
                ), "dcst", 16)
        blk.gpsimd(cst)
    cst_tok = tk.tok("dcst")

    class _SameBlk:
        def __init__(self, b):
            self.b = b

        def __enter__(self):
            return self.b

        def __exit__(self, *a):
            return False

    _mega = nc.Block()
    _megah = _mega.__enter__()
    BLK = lambda: _SameBlk(_megah)

    wcons = {"dwa": None, "dwb": None, "dwc": None, "dwd": None}
    pool_prev = [None]
    out_slot_tok = [None] * 4
    xres_slot_tok = [None] * 4
    xb_cons_dv = [None, None]
    xb_cons_ac = [None, None]
    xnt_cons = [None]

    mixt_ctx = contextlib.ExitStack()
    MIXT = mixt_ctx.enter_context(nc.sbuf_tensor("MIXT", [128, HQL * t_batch], BF16))

    bufs = {}

    def emit_ln_chunk(blk, xsrc, row0, pT, pT_free, extra_war=()):
        XBa, XBb = bufs["XBa"], bufs["XBb"]
        XNa, XNb = bufs["XNa"], bufs["XNb"]
        for ts in range(NT):
            xn = XNa if ts % 2 == 0 else XNb
            xb = XBa if ts % 2 == 0 else XBb
            dxs = "dxa" if ts % 2 == 0 else "dxb"
            st = {}

            def p0ld(sp, ts=ts, xb=xb, dxs=dxs):
                tk.wait(sp, "sp", tk.tok("dbg"), *extra_war)
                tk.wait(sp, "sp", *out_slot_tok, *xres_slot_tok)
                tk.wait(sp, "sp", xb_cons_dv[ts % 2], xb_cons_ac[ts % 2])
                tk.inc(sp.dma_start(
                    xb[:, :],
                    bass.AP(xsrc, (row0 + ts * 128) * DM, [[DM, 128], [1, DM]]),
                ), dxs, 16)
                st["x"] = tk.tok(dxs)
            blk.sync(p0ld)
            xsl = xb[:, :]

            def p0a(a, ts=ts, xsl=xsl, xn=xn):
                tk.wait(a, ACT, st["x"], cst_tok, *extra_war)
                tk.wait(a, ACT, xb_cons_dv[ts % 2], xb_cons_ac[ts % 2])
                st["sq"] = tk.inc(a.activation(xn[:, :], xsl, AF.Square,
                                               accum_out=rcol(ts, C_SUMSQ)), "ac")
            blk.scalar(p0a)

            def p0v(v, ts=ts, xsl=xsl):
                tk.wait(v, DVE, st["x"], *extra_war)
                tk.wait(v, DVE, tk.tok("dv"))
                tk.inc(v.tensor_reduce(rcol(ts, C_SUM), xsl, axis=AX.X, op=ALU.add), "dv")
                tk.wait(v, DVE, tk.tok("dv"))
                tk.inc(v.tensor_scalar_mul(rcol(ts, C_MEAN), rcol(ts, C_SUM), 1.0 / DM), "dv")
                tk.wait(v, DVE, tk.tok("dv"))
                tk.inc(v.tensor_mul(rcol(ts, C_M2), rcol(ts, C_MEAN), rcol(ts, C_MEAN)), "dv")
                tk.wait(v, DVE, st["sq"])
                tk.wait(v, DVE, tk.tok("dv"))
                tk.inc(v.tensor_scalar(rcol(ts, C_VAR), rcol(ts, C_SUMSQ),
                                       1.0 / DM, EPS_LN, op0=ALU.mult, op1=ALU.add), "dv")
                tk.wait(v, DVE, tk.tok("dv"))
                st["var"] = tk.inc(v.tensor_sub(
                    rcol(ts, C_VAR), rcol(ts, C_VAR), rcol(ts, C_M2)), "dv")
            blk.vector(p0v)

            def p0a2(a, ts=ts):
                tk.wait(a, ACT, st["var"])
                st["sv"] = tk.inc(a.activation(rcol(ts, C_SV), rcol(ts, C_VAR), AF.Sqrt), "ac")
            blk.scalar(p0a2)

            def p0v2(v, ts=ts, xn=xn, xsl=xsl):
                tk.wait(v, DVE, st["sv"], xnt_cons[0])
                tk.wait(v, DVE, tk.tok("dv"))
                tk.inc(v.reciprocal(rcol(ts, C_RSTD), rcol(ts, C_SV)), "dv")
                tk.wait(v, DVE, tk.tok("dv"))
                st["xn"] = tk.inc(v.tensor_scalar(
                    xn[:, :], xsl, rcol(ts, C_MEAN), rcol(ts, C_RSTD),
                    op0=ALU.subtract, op1=ALU.mult), "dv")
                xb_cons_dv[ts % 2] = st["xn"]
                xb_cons_ac[ts % 2] = st["sq"]
            blk.vector(p0v2)

            for half in range(2):
                bank = pT[half % len(pT)]
                bid = half % len(pT)

                def p0t(t, ts=ts, half=half, xn=xn, bank=bank, bid=bid):
                    tk.wait(t, "pe", st["xn"], cst_tok, pT_free[bid])
                    ins = None
                    for k in range(8):
                        kk = half * 8 + k
                        ins = t.transpose(bank[:, k * 128:(k + 1) * 128],
                                          xn[:, kk * 128:(kk + 1) * 128], IDENT[:, :])
                    st[f"tr{half}"] = tk.inc(ins, "pe")
                blk.tensor(p0t)

                def p0ea(a, ts=ts, half=half, bank=bank, bid=bid):
                    tk.wait(a, ACT, st[f"tr{half}"], xnt_cons[0])
                    pT_free[bid] = tk.inc(a.copy(
                        bass.AP(XNT, half * 8 * t_chunk + ts * 128,
                                [[NDT * t_chunk, 128], [t_chunk, 8], [1, 128]]),
                        bass.AP(bank, 0, [[1024, 128], [128, 8], [1, 128]])), "ac")
                blk.scalar(p0ea)

    actx0 = contextlib.ExitStack()
    asb0 = lambda nm, shp, dt: actx0.enter_context(nc.sbuf_tensor(nm, shp, dt))
    bufs["XBa"] = asb0("XBa", [128, DM], F32)
    bufs["XBb"] = asb0("XBb", [128, DM], F32)
    bufs["XNa"] = asb0("XNa", [128, DM], BF16)
    bufs["XNb"] = asb0("XNb", [128, DM], BF16)
    QB = asb0("QB", [128, NT * HQL * DH], BF16)
    KB = asb0("KB", [128, NT * HKV * DH], BF16)
    VB = asb0("VB", [128, NT * HKV * DH], BF16)
    TMP1 = asb0("TMP1", [128, HQL * DH], BF16)
    TMP2 = asb0("TMP2", [128, HKV * DH], BF16)
    QKT = asb0("QKT", [128, (HQL + HKV) * 128], BF16)
    QP = asb0("QP", [128, NT * HQL * DH], BF16)
    KP = asb0("KP", [128, NT * HKV * DH], BF16)
    ATMP = TMP1
    PSCR = asb0("PSCR", [128, 1024], BF16)
    AB = ROWS[:, 96:160]
    SBW = NSUB * DM
    QT2_OFF, KT2_OFF = 0, 1024
    KROT_OFF = 6144
    P0Q_OFF, P0K_OFF = 2048, 4096
    MIX_OFF = 10240

    pctx = contextlib.ExitStack()
    pT = [pctx.enter_context(nc.psum_tensor(f"pT{i}", [128, 1024], BF16)) for i in range(2)]
    pT_free = {0: None, 1: None}
    fb_free = {0: None, 1: None, 4: None, 5: None}
    p4_free = [None]
    prev_sin = [None]
    sin_slot_prev = [None] * NT
    feat_prev = [None]
    p1_last = [None]
    evts = {}
    stss = {}
    pool_prev = [None]

    p4evt = {}

    def emit_p4g(blk, dc):
        buf, sem = (WA, "dwa") if dc % 2 == 0 else (WB, "dwb")

        def p4g(g, dc=dc, buf=buf, sem=sem):
            tk.wait(g, "pl", tk.tok("dbg"))
            tk.wait(g, "pl", wcons[sem])
            tk.inc(g.dma_start(
                bass.AP(buf, 0, [[8192, 128], [1, 8192]]),
                bass.AP(WOT, dc * 128 * 8192, [[8192, 128], [1, 8192]]),
            ), sem, 16)
            p4evt[("w", dc)] = tk.tok(sem)
        blk.gpsimd(p4g)

    jobs = ([(WQTS, BQs, QB, oc) for oc in range(2)]
            + [(WKT, BKs, KB, oc) for oc in range(2)]
            + [(WVT, BVs, VB, oc) for oc in range(2)])

    cast_list = [(wd, ws, j)
                 for ws, wd in ((W1T, W1Bd), (W3T, W3Bd), (W2T, W2Bd))
                 for j in range(32)]
    cast_ptr = [0]

    def emit_casts(g, kmax=6, limit=96):
        for _ in range(kmax):
            if cast_ptr[0] >= min(len(cast_list), limit):
                return
            wd, ws, j = cast_list[cast_ptr[0]]
            cast_ptr[0] += 1
            tk.inc(g.dma_start(
                bass.AP(wd, j * 128 * 4096, [[4096, 128], [1, 4096]]),
                bass.AP(ws, j * 128 * 4096, [[4096, 128], [1, 4096]]),
            ), "dwe", 16)

    def emit_p1g(blk, c, ji):
        evt = evts[c]
        wd, bs, ob, oc = jobs[ji]
        buf, sem = (WA, "dwa") if ji % 2 == 0 else (WB, "dwb")

        def p1g(g, buf=buf, sem=sem, wd=wd, oc=oc, ji=ji, evt=evt):
            tk.wait(g, "pl", wcons[sem])
            tk.inc(g.dma_start(
                bass.AP(buf, 0, [[8192, 128], [1, 8192]]),
                bass.AP(wd, oc * 128 * 8192, [[8192, 128], [1, 8192]]),
            ), sem, 16)
            evt[("w", ji)] = tk.tok(sem)
            if c > 0:
                emit_casts(g, kmax=4, limit=64)
        blk.gpsimd(p1g)

    def emit_p1t(blk, c, ji):
        evt = evts[c]
        wd, bs, ob, oc = jobs[ji]
        buf, sem = (WA, "dwa") if ji % 2 == 0 else (WB, "dwb")

        def p1t(t, ji=ji, buf=buf, bs=bs, oc=oc, sem=sem, evt=evt):
            tk.wait(t, "pe", evt[("w", ji)], pT_free[0], pT_free[1],
                    feat_prev[0], p1_last[0])
            for ts in range(NT):
                bank = P[ts % 4]
                if (ji - 1, ts) in evt:
                    tk.wait(t, "pe", evt[(ji - 1, ts)])
                t.matmul(bank[:, :], ONES[:, :], bs[:, oc * 512:(oc + 1) * 512],
                         start=True, stop=False)
                last = None
                for k in range(NDT):
                    last = t.matmul(
                        bank[:, :],
                        XNT[:, k * t_chunk + ts * 128: k * t_chunk + (ts + 1) * 128],
                        buf[:, k * 512:(k + 1) * 512],
                        start=False, stop=(k == NDT - 1))
                evt[("mm", ji, ts)] = tk.inc(last, "pe")
            wcons[sem] = tk.tok("pe")
        blk.tensor(p1t)
        if ji + 2 < 6:
            emit_p1g(blk, c, ji + 2)

    def p1a_ops(a, c, ji):
        evt = evts[c]
        prev_sts = stss.get(c - 1)
        wd, bs, ob, oc = jobs[ji]
        tk.wait(a, ACT, tk.tok("dbg"))
        for ts in range(NT):
            bank = P[ts % 4]
            tk.wait(a, ACT, evt[("mm", ji, ts)])
            if prev_sts is not None:
                tk.wait(a, ACT, prev_sts[ts]["tr"], prev_sts[ts]["mx"])
            evt[(ji, ts)] = tk.inc(a.copy(
                bass.AP(ob, ts * 1024 + oc * 512, [[NT * 1024, 128], [1, 512]]),
                bank[:, :]), "ac")

    def emit_p1_main(blk, c, full=False):
        evts[c] = {}
        emit_p1g(blk, c, 0)
        emit_p1g(blk, c, 1)
        emit_p1t(blk, c, 0)
        if full:
            for ji in range(1, 5):
                blk.scalar(lambda a, ji=ji: p1a_ops(a, c, ji - 1))
                emit_p1t(blk, c, ji)

    def emit_p1_tail(blk, c, mixc):
        evt = evts[c]

        def p1av(a, evt=evt, mixc=mixc):
            tk.wait(a, ACT, tk.tok("dbg"))
            for ts in range(NT):
                bank = P[ts % 4]
                tk.wait(a, ACT, evt[("mm", 4, ts)])
                if mixc is not None:
                    tk.wait(a, ACT, *mixc[ts])
                evt[(4, ts)] = tk.inc(a.copy(
                    bass.AP(VB, ts * 1024, [[NT * 1024, 128], [1, 512]]),
                    bank[:, :]), "ac")
        blk.scalar(p1av)

        def p1tv(t, evt=evt):
            tk.wait(t, "pe", evt[("w", 5)])
            for ts in range(NT):
                bank = P[ts % 4]
                tk.wait(t, "pe", evt[(4, ts)])
                t.matmul(bank[:, :], ONES[:, :], BVs[:, 512:1024], start=True, stop=False)
                last = None
                for k in range(NDT):
                    last = t.matmul(
                        bank[:, :],
                        XNT[:, k * t_chunk + ts * 128: k * t_chunk + (ts + 1) * 128],
                        WB[:, k * 512:(k + 1) * 512],
                        start=False, stop=(k == NDT - 1))
                evt[("mm", 5, ts)] = tk.inc(last, "pe")
            wcons["dwb"] = tk.tok("pe")
        blk.tensor(p1tv)

        def p1av2(a, evt=evt):
            for ts in range(NT):
                bank = P[ts % 4]
                tk.wait(a, ACT, evt[("mm", 5, ts)])
                evt[(5, ts)] = tk.inc(a.copy(
                    bass.AP(VB, ts * 1024 + 512, [[NT * 1024, 128], [1, 512]]),
                    bank[:, :]), "ac")
        blk.scalar(p1av2)
        xnt_cons[0] = tk.tok("pe")
        p1_last[0] = evt[(5, NT - 1)]

    for ac in range(n_ac):
        arow0 = ac * t_chunk

        if ac == 0:
            with BLK() as blk:
                emit_ln_chunk(blk, XBAT, 0, [pT[0]], pT_free)
                emit_p1_main(blk, 0, full=True)
                emit_p1_tail(blk, 0, None)
        evt = evts[ac]

        sts2 = stss[ac] = {}
        mix_toks = {}
        a_done = {}
        with BLK() as blk:
            for ts in range(NT):
                slot = ac * NT + ts
                st = sts2[ts] = {}

                def p2ropek(g, ts=ts, slot=slot, st=st, evt=evt):
                    tk.wait(g, "pl", tk.tok("po"))
                    tk.wait(g, "pl", evt.get((2, ts)), evt.get((3, ts)))
                    base = ts * HKV * DH
                    kap = bass.AP(KB, base, [[NT * HKV * DH, 128], [128, HKV], [1, 128]])
                    cosap = bass.AP(COSB, slot * 128, [[t_batch, 128], [0, HKV], [1, 128]])
                    kswap = bass.AP(KB, base + 64,
                                    [[NT * HKV * DH, 128], [128, HKV], [-64, 2], [1, 64]])
                    sinap = bass.AP(SINSB, slot * 128,
                                    [[t_batch, 128], [0, HKV], [64, 2], [1, 64]])
                    t1 = bass.AP(SBf, KROT_OFF, [[SBW, 128], [128, HKV], [1, 128]])
                    t2 = bass.AP(SBf, KROT_OFF + 1024,
                                 [[SBW, 128], [128, HKV], [64, 2], [1, 64]])
                    t2r = bass.AP(SBf, KROT_OFF + 1024, [[SBW, 128], [128, HKV], [1, 128]])
                    tk.inc(g.tensor_tensor(t1, kap, cosap, op=ALU.mult), "po")
                    tk.wait(g, "pl", tk.tok("po"))
                    tk.inc(g.tensor_tensor(t2, kswap, sinap, op=ALU.mult), "po")
                    tk.wait(g, "pl", tk.tok("po"))
                    st["rotk"] = tk.inc(g.tensor_tensor(kap, t1, t2r, op=ALU.add), "po")
                blk.gpsimd(p2ropek)

                def p2rope(v, ts=ts, slot=slot, st=st, evt=evt):
                    tk.wait(v, DVE, tk.tok("dv"))
                    for ji in range(2):
                        tk.wait(v, DVE, evt.get((ji, ts)))
                    base = ts * HQL * DH
                    qap = bass.AP(QB, base, [[NT * HQL * DH, 128], [128, HQL], [1, 128]])
                    cosap = bass.AP(COSB, slot * 128, [[t_batch, 128], [0, HQL], [1, 128]])
                    qswap = bass.AP(QB, base + 64,
                                    [[NT * HQL * DH, 128], [128, HQL], [-64, 2], [1, 64]])
                    sinap = bass.AP(SINSB, slot * 128,
                                    [[t_batch, 128], [0, HQL], [64, 2], [1, 64]])
                    t1 = bass.AP(TMP1, 0, [[HQL * DH, 128], [128, HQL], [1, 128]])
                    t2 = bass.AP(SBf, QT2_OFF, [[SBW, 128], [128, HQL], [64, 2], [1, 64]])
                    t2r = bass.AP(SBf, QT2_OFF, [[SBW, 128], [128, HQL], [1, 128]])
                    v.tensor_tensor(t1, qap, cosap, op=ALU.mult)
                    tk.inc(v.tensor_tensor(t2, qswap, sinap, op=ALU.mult), "dv")
                    tk.wait(v, DVE, tk.tok("dv"))
                    st["rot"] = tk.inc(v.tensor_tensor(qap, t1, t2r, op=ALU.add), "dv")
                    tk.wait(v, DVE, tk.tok("dv"))
                    flatq = bass.AP(QB, base, [[NT * HQL * DH, 128], [1, HQL * DH]])
                    tk.inc(v.tensor_tensor(TMP1[:, :], flatq, flatq, op=ALU.mult), "dv")
                    tk.wait(v, DVE, st["rotk"])
                    baske = ts * HKV * DH
                    flatk = bass.AP(KB, baske, [[NT * HKV * DH, 128], [1, HKV * DH]])
                    tk.inc(v.tensor_tensor(TMP2[:, :], flatk, flatk, op=ALU.mult), "dv")
                    tk.wait(v, DVE, tk.tok("dv"))
                    for coff, tmp in ((C_NQ, TMP1), (C_NQ + HQL, TMP2)):
                        ins = v.tensor_reduce(
                            rcol(ts, coff, HQL),
                            bass.AP(tmp, 0, [[HQL * DH, 128], [128, HQL], [1, 128]]),
                            axis=AX.X, op=ALU.add)
                    tk.inc(ins, "dv")
                    tk.wait(v, DVE, tk.tok("dv"))
                    st["mx"] = tk.inc(v.tensor_scalar_max(
                        rcol(ts, C_NQ, HQL + HKV), rcol(ts, C_NQ, HQL + HKV), 1e-12), "dv")
                blk.vector(p2rope)

                def p2tr(t, ts=ts, st=st):
                    tk.wait(t, "pe", st["rot"], st["rotk"], pT_free[0], pT_free[1])
                    for gi in range(2):
                        bank = pT[gi]
                        for k in range(8):
                            if gi == 0:
                                srcq = bass.AP(QB, ts * HQL * DH + k * 128,
                                               [[NT * HQL * DH, 128], [1, 128]])
                            else:
                                srcq = bass.AP(KB, ts * HKV * DH + k * 128,
                                               [[NT * HKV * DH, 128], [1, 128]])
                            ins = t.transpose(bank[:, k * 128:(k + 1) * 128], srcq,
                                              IDENT[:, :])
                    st["tr"] = tk.inc(ins, "pe")
                blk.tensor(p2tr)

                def p2te(a, ts=ts, st=st):
                    tk.wait(a, ACT, st["tr"], prev_sin[0])
                    if ts > 0:
                        tk.wait(a, ACT, sts2[ts - 1]["fm"])
                    for gi in range(2):
                        ins = a.copy(QKT[:, gi * 1024:(gi + 1) * 1024], pT[gi][:, :])
                    st["te"] = tk.inc(ins, "ac")
                    pT_free[0] = pT_free[1] = st["te"]
                blk.scalar(p2te)

                fbanks = (4, 5) if ts % 2 == 0 else (0, 1)

                def p2fm(t, ts=ts, st=st, fbanks=fbanks, evt=evt):
                    tk.wait(t, "pe", st["te"], fb_free[fbanks[0]], fb_free[fbanks[1]])
                    if ts % 2 == 1:
                        tk.wait(t, "pe", evt.get((5, 0)), evt.get((5, 1)))
                    for gi in range(2):
                        fb = P[fbanks[gi]]
                        for k in range(8):
                            ins = t.matmul(fb[:, k * 64:(k + 1) * 64],
                                           QKT[:, (gi * 8 + k) * 128:(gi * 8 + k + 1) * 128],
                                           RT16[:, :], start=True, stop=True)
                    st["fm"] = tk.inc(ins, "pe")
                blk.tensor(p2fm)

                def p2fe(a, ts=ts, st=st, fbanks=fbanks):
                    tk.wait(a, ACT, st["fm"], sin_slot_prev[ts])
                    for gi in range(2):
                        off = (P0Q_OFF if gi == 0 else P0K_OFF) + ts * 512
                        ins = a.copy(
                            bass.AP(SBf, off, [[SBW, 128], [1, 512]]),
                            P[fbanks[gi]][:, :])
                    st["fe"] = tk.inc(ins, "ac")
                    fb_free[fbanks[0]] = fb_free[fbanks[1]] = st["fe"]
                blk.scalar(p2fe)
            feat_prev[0] = sts2[NT - 1]["fe"]

            if debug and ac == 0:
                def pdbgr(g):
                    tk.wait(g, "pl", sts2[NT - 1]["mx"])
                    tk.inc(g.dma_start(dbg["DBG_QROT"].ap(), QB[:, :]), "dbg", 16)
                blk.gpsimd(pdbgr)

            bt2 = {}

            def p2sqr(a):
                tk.wait(a, ACT, sts2[NT - 1]["mx"])
                nap = bass.AP(ROWS, C_NQ, [[160, 128], [24, NT], [1, HQL + HKV]])
                bt2["sqrt"] = tk.inc(a.activation(nap, nap, AF.Sqrt), "ac")
            blk.scalar(p2sqr)

            def p2inv(v):
                tk.wait(v, DVE, bt2["sqrt"])
                nap = bass.AP(ROWS, C_NQ, [[160, 128], [24, NT], [1, HQL + HKV]])
                bt2["inv"] = tk.inc(v.reciprocal(nap, nap), "dv")
            blk.vector(p2inv)

        if ac + 1 < n_ac:
            with BLK() as blk:
                emit_ln_chunk(blk, XBAT, (ac + 1) * t_chunk, [pT[0]], pT_free)
                emit_p1_main(blk, ac + 1)

        HSPL = 4
        nxt = ac + 1 if ac + 1 < n_ac else None
        pool_mix_tok = {}
        mixdv = {}

        def emit_p3m(blk, ts2):
            def p3m(v, ts2=ts2):
                tk.wait(v, DVE, tk.tok("dv"))
                tk.wait(v, DVE, evt.get((4, ts2)), evt.get((5, ts2)))
                vfull = bass.AP(VB, ts2 * HKV * DH, [[NT * HKV * DH, 128], [1, 1024]])
                for hp in range(0, HSPL, 2):
                    tk.wait(v, DVE, tk.tok("dv"))
                    for hh, tmp in ((hp, TMP1), (hp + 1, TMP2)):
                        abb = bass.AP(ROWS, 96 + hh * 8, [[160, 128], [1, 8], [0, 128]])
                        ins = v.tensor_tensor(tmp[:, :], vfull, abb, op=ALU.mult)
                    tk.inc(ins, "dv")
                    for w in (512, 256):
                        tk.wait(v, DVE, tk.tok("dv"))
                        for tmp in (TMP1, TMP2):
                            ins = v.tensor_tensor(tmp[:, :w], tmp[:, :w],
                                                  tmp[:, w:2 * w], op=ALU.add)
                        tk.inc(ins, "dv")
                    tk.wait(v, DVE, tk.tok("dv"))
                    for hh, tmp in ((hp, TMP1), (hp + 1, TMP2)):
                        msl = bass.AP(SBf, MIX_OFF + ts2 * HQL * DH + hh * 128,
                                      [[SBW, 128], [1, 128]])
                        ins = v.tensor_tensor(msl, tmp[:, 0:128],
                                              tmp[:, 128:256], op=ALU.add)
                    tk.inc(ins, "dv")
            blk.vector(p3m)
            mixdv[ts2] = tk.tok("dv")
            mix_toks[ts2] = (mixdv[ts2], pool_mix_tok[ts2])

            def p4tr(t, ts2=ts2):
                tk.wait(t, "pe", *mix_toks[ts2])
                tk.wait(t, "pe", pT_free[1] if ts2 == 0 else None, p4_free[0])
                ins = None
                for h in range(HQL):
                    ins = t.transpose(
                        pT[1][:, h * 128:(h + 1) * 128],
                        bass.AP(SBf, MIX_OFF + ts2 * HQL * DH + h * 128,
                                [[SBW, 128], [1, 128]]),
                        IDENT[:, :])
                sts2[ts2]["p4tr"] = tk.inc(ins, "pe")
            blk.tensor(p4tr)

        def p4te_ops(a, ts2):
            tk.wait(a, ACT, sts2[ts2]["p4tr"])
            p4_free[0] = tk.inc(a.copy(
                bass.AP(MIXT, arow0 + ts2 * 128,
                        [[HQL * t_batch, 128], [t_batch, HQL], [1, 128]]),
                bass.AP(pT[1], 0, [[1024, 128], [128, HQL], [1, 128]])), "ac")

        with BLK() as blk:
            for ts in range(NT):
                st = sts2[ts]
                cs = ts % 2

                def p2wrap(v, ts=ts, st=st, cs=cs):
                    tk.wait(v, DVE, tk.tok("dv"))
                    tk.wait(v, DVE, bt2["inv"], st["fe"], sts2[NT - 1]["fm"])
                    if ts >= 2:
                        tk.wait(v, DVE, sts2[ts - 2]["sin"])
                    for poff, nh, coff in [(P0Q_OFF, HQL, C_NQ),
                                           (P0K_OFF, HKV, C_NQ + HQL)]:
                        sap = bass.AP(SBf, poff + ts * 512, [[SBW, 128], [64, nh], [1, 64]])
                        iap = bass.AP(ROWS, ts * 24 + coff, [[160, 128], [1, nh], [0, 64]])
                        ins = v.tensor_tensor(sap, sap, iap, op=ALU.mult)
                    tk.inc(ins, "dv")
                    fq = bass.AP(SBf, P0Q_OFF + ts * 512, [[SBW, 128], [1, 512]])
                    fk = bass.AP(SBf, P0K_OFF + ts * 512, [[SBW, 128], [1, 512]])
                    Cq = QKT[:, cs * 1024: cs * 1024 + 512]
                    Ck = QKT[:, cs * 1024 + 512: cs * 1024 + 1024]
                    t1, t2 = TMP1[:, 0:512], TMP1[:, 512:1024]
                    t3, t4 = TMP2[:, 0:512], TMP2[:, 512:1024]
                    u2 = bass.AP(SBf, QT2_OFF, [[SBW, 128], [1, 512]])
                    u4 = bass.AP(SBf, QT2_OFF + 512, [[SBW, 128], [1, 512]])
                    tk.wait(v, DVE, tk.tok("dv"))
                    v.tensor_scalar_add(Cq, fq, PI / 2)
                    v.tensor_scalar_add(Ck, fk, PI / 2)
                    v.tensor_scalar(t1, fq, -PI, 2 * PI, op0=ALU.is_lt, op1=ALU.mult)
                    v.tensor_scalar(t2, fq, PI, 2 * PI, op0=ALU.is_gt, op1=ALU.mult)
                    v.tensor_scalar(t3, fk, -PI, 2 * PI, op0=ALU.is_lt, op1=ALU.mult)
                    tk.inc(v.tensor_scalar(t4, fk, PI, 2 * PI,
                                           op0=ALU.is_gt, op1=ALU.mult), "dv")
                    tk.wait(v, DVE, tk.tok("dv"))
                    v.tensor_tensor(fq, fq, t1, op=ALU.add)
                    tk.inc(v.tensor_tensor(fk, fk, t3, op=ALU.add), "dv")
                    tk.wait(v, DVE, tk.tok("dv"))
                    v.tensor_tensor(fq, fq, t2, op=ALU.subtract)
                    v.tensor_tensor(fk, fk, t4, op=ALU.subtract)
                    v.tensor_scalar(t1, Cq, -PI, 2 * PI, op0=ALU.is_lt, op1=ALU.mult)
                    v.tensor_scalar(t3, Ck, -PI, 2 * PI, op0=ALU.is_lt, op1=ALU.mult)
                    v.tensor_scalar(u2, Cq, PI, 2 * PI, op0=ALU.is_gt, op1=ALU.mult)
                    tk.inc(v.tensor_scalar(u4, Ck, PI, 2 * PI,
                                           op0=ALU.is_gt, op1=ALU.mult), "dv")
                    tk.wait(v, DVE, tk.tok("dv"))
                    v.tensor_tensor(Cq, Cq, t1, op=ALU.add)
                    tk.inc(v.tensor_tensor(Ck, Ck, t3, op=ALU.add), "dv")
                    tk.wait(v, DVE, tk.tok("dv"))
                    v.tensor_tensor(Cq, Cq, u2, op=ALU.subtract)
                    st["wrap"] = tk.inc(v.tensor_tensor(Ck, Ck, u4, op=ALU.subtract), "dv")
                blk.vector(p2wrap)

                if ts > 0:
                    emit_p3m(blk, ts - 1)

                def p2sin(a, ts=ts, st=st, cs=cs):
                    tk.wait(a, ACT, st["wrap"], tk.tok("dbg"))
                    for nm2, nh in [("q", HQL), ("k", HKV)]:
                        dstb = QP if nm2 == "q" else KP
                        poff = P0Q_OFF if nm2 == "q" else P0K_OFF
                        dwid = nh * DH
                        coff = cs * 1024 + (0 if nm2 == "q" else 512)
                        a.activation(
                            bass.AP(dstb, ts * dwid, [[NT * dwid, 128], [128, nh], [1, 64]]),
                            bass.AP(SBf, poff + ts * 512, [[SBW, 128], [64, nh], [1, 64]]),
                            AF.Sin)
                        ins = a.activation(
                            bass.AP(dstb, ts * dwid + 64,
                                    [[NT * dwid, 128], [128, nh], [1, 64]]),
                            bass.AP(QKT, coff, [[2048, 128], [64, nh], [1, 64]]),
                            AF.Sin)
                    st["sin"] = tk.inc(ins, "ac")
                    prev_sin[0] = st["sin"]
                    sin_slot_prev[ts] = st["sin"]
                    if ts > 0:
                        p4te_ops(a, ts - 1)
                    if nxt is not None:
                        p1a_ops(a, nxt, ts)
                blk.scalar(p2sin)

                def p3a(v, ts=ts, st=st):
                    tk.wait(v, DVE, tk.tok("dv"))
                    tk.wait(v, DVE, tk.tok("dbg"), pool_prev[0])
                    tk.wait(v, DVE, st["sin"])
                    for h in range(HQL):
                        if h > 0:
                            tk.wait(v, DVE, tk.tok("dv"))
                        for j in range(HKV):
                            qsl = bass.AP(QP, ts * HQL * DH + h * 128,
                                          [[NT * HQL * DH, 128], [1, 128]])
                            ksl = bass.AP(KP, ts * HKV * DH + j * 128,
                                          [[NT * HKV * DH, 128], [1, 128]])
                            ins = v.scalar_tensor_tensor(
                                ATMP[:, j * 128:(j + 1) * 128], qsl, 1.0 / 64.0, ksl,
                                op0=ALU.mult, op1=ALU.mult,
                                accum_out=ROWS[:, 96 + h * 8 + j:97 + h * 8 + j])
                        tk.inc(ins, "dv")
                    a_done[ts] = tk.tok("dv")
                blk.vector(p3a)

                if nxt is not None:
                    emit_p1t(blk, nxt, ts + 1)
                elif ts < 2:
                    emit_p4g(blk, ts)

                def p3p(g, ts=ts, st=st):
                    tk.wait(g, "pl", a_done[ts])
                    tk.wait(g, "pl", evt.get((4, ts)), evt.get((5, ts)))
                    vfull = bass.AP(VB, ts * HKV * DH, [[NT * HKV * DH, 128], [1, 1024]])
                    for h in range(HSPL, HQL):
                        abb = bass.AP(ROWS, 96 + h * 8, [[160, 128], [1, 8], [0, 128]])
                        tk.wait(g, "pl", tk.tok("po"))
                        tk.inc(g.tensor_tensor(PSCR[:, :], vfull, abb, op=ALU.mult), "po")
                        tk.wait(g, "pl", tk.tok("po"))
                        tk.inc(g.tensor_tensor(PSCR[:, 0:512], PSCR[:, 0:512],
                                               PSCR[:, 512:1024], op=ALU.add), "po")
                        tk.wait(g, "pl", tk.tok("po"))
                        tk.inc(g.tensor_tensor(PSCR[:, 0:256], PSCR[:, 0:256],
                                               PSCR[:, 256:512], op=ALU.add), "po")
                        msl = bass.AP(SBf, MIX_OFF + ts * HQL * DH + h * 128,
                                      [[SBW, 128], [1, 128]])
                        tk.wait(g, "pl", tk.tok("po"))
                        tk.inc(g.tensor_tensor(msl, PSCR[:, 0:128],
                                               PSCR[:, 128:256], op=ALU.add), "po")
                blk.gpsimd(p3p)
                pool_prev[0] = tk.tok("po")
                pool_mix_tok[ts] = pool_prev[0]
                if debug and ac == 0 and ts == 0:
                    def p3d(g):
                        tk.wait(g, "pl", tk.tok("dv"), tk.tok("po"))
                        tk.inc(g.dma_start(dbg["DBG_A"].ap(), ROWS[:, 96:160]), "dbg", 16)
                    blk.gpsimd(p3d)

            emit_p3m(blk, NT - 1)

            def p4te3(a):
                p4te_ops(a, NT - 1)
            blk.scalar(p4te3)
        pT_free[1] = p4_free[0]

        if ac + 1 < n_ac:
            with BLK() as blk:
                emit_p1_tail(blk, ac + 1, mix_toks)

        if debug and ac == 0:
            with BLK() as blk:
                def pdbg(g):
                    tk.wait(g, "pl", tk.tok("dv"), tk.tok("ac"))
                    for nm2, srcb in [("DBG_QP", QP), ("DBG_KP", KP)]:
                        tk.inc(g.dma_start(dbg[nm2].ap(), srcb[:, :]), "dbg", 16)
                blk.gpsimd(pdbg)

    if debug:
        with BLK() as blk:
            def pdbg2(g):
                tk.wait(g, "pl", tk.tok("dv"), tk.tok("ac"))
                tk.inc(g.dma_start(dbg["DBG_MIXT"].ap(), MIXT[:, :]), "dbg", 16)
            blk.gpsimd(pdbg2)

    pctx.close()
    actx0.close()

    attn_done = [tk.tok("dv"), tk.tok("ac"), tk.tok("po")]
    with BLK() as blk:
        bank_tok = {}
        for dc in range(4):
            buf, sem = (WA, "dwa") if dc % 2 == 0 else (WB, "dwb")
            if dc >= 2:
                emit_p4g(blk, dc)

            for r in range(NSUB):
                def p4t(t, dc=dc, r=r, buf=buf):
                    tk.wait(t, "pe", p4evt[("w", dc)], bank_tok.get(r % 4), *attn_done)
                    bank = P[r % 4]
                    t.matmul(bank[:, :], ONES[:, :], BOs[:, dc * 512:(dc + 1) * 512],
                             start=True, stop=False)
                    last = None
                    for m in range(16):
                        lhsT = bass.AP(MIXT, r * t_batch + m,
                                       [[HQL * t_batch, 128], [16, 128]])
                        last = t.matmul(bank[:, :], lhsT,
                                        buf[:, m * 512:(m + 1) * 512],
                                        start=False, stop=(m == 15))
                    p4evt[("mm", dc, r)] = tk.inc(last, "pe")
                blk.tensor(p4t)

                if r % 2 == 0:
                    def p4v(v, dc=dc, r=r):
                        tk.wait(v, DVE, p4evt[("mm", dc, r)])
                        tk.wait(v, DVE, tk.tok("dv"))
                        bank_tok[r % 4] = tk.inc(v.tensor_copy(
                            bass.AP(SBf, r * DM + dc * 512, [[NSUB * DM, 128], [1, 512]]),
                            P[r % 4][:, :]), "dv")
                    blk.vector(p4v)
                else:
                    def p4a(a, dc=dc, r=r):
                        tk.wait(a, ACT, p4evt[("mm", dc, r)])
                        tk.wait(a, ACT, tk.tok("ac"))
                        bank_tok[r % 4] = tk.inc(a.copy(
                            bass.AP(SBf, r * DM + dc * 512, [[NSUB * DM, 128], [1, 512]]),
                            P[r % 4][:, :]), "ac")
                    blk.scalar(p4a)

            def p4w(t, dc=dc, sem=sem):
                pass
            wcons[sem] = tk.tok("pe")

    p4b_done = [tk.tok("pe"), tk.tok("dv"), tk.tok("ac")]
    mixt_ctx.close()

    if debug:
        with BLK() as blk:
            def pdbg3(g):
                tk.wait(g, "pl", *p4b_done)
                tk.inc(g.dma_start(dbg["DBG_S"].ap(), SBf[:, :]), "dbg", 16)
            blk.gpsimd(pdbg3)

    fctx0 = contextlib.ExitStack()
    fsb0 = lambda nm, shp, dt: fctx0.enter_context(nc.sbuf_tensor(nm, shp, dt))
    U = fsb0("U", [128, NA * t_chunk], BF16)
    GT8 = fsb0("GT8", [128, NT * 2 * 512], BF16)
    TMPR = fsb0("TMPR", [128, 512], BF16)
    xb_cons_dv[0] = xb_cons_dv[1] = None
    xb_cons_ac[0] = xb_cons_ac[1] = None
    ffn_prev = []
    p5_war = list(p4b_done)
    s1_free = [()]
    s2_free = [()]

    for oc2 in range(n_oc):
        orow0 = oc2 * t_chunk

        p0ctx = contextlib.ExitStack()
        bufs["XBa"] = p0ctx.enter_context(nc.sbuf_tensor(f"XBaf_{oc2}", [128, DM], F32))
        bufs["XBb"] = p0ctx.enter_context(nc.sbuf_tensor(f"XBbf_{oc2}", [128, DM], F32))
        bufs["XNa"] = p0ctx.enter_context(nc.sbuf_tensor(f"XNaf_{oc2}", [128, DM], BF16))
        bufs["XNb"] = p0ctx.enter_context(nc.sbuf_tensor(f"XNbf_{oc2}", [128, DM], BF16))
        pctx = contextlib.ExitStack()
        pT = [pctx.enter_context(nc.psum_tensor(f"pTf{i}_{oc2}", [128, 1024], BF16))
              for i in range(2)]
        pT_free = {0: None, 1: None}
        with BLK() as blk:
            emit_ln_chunk(blk, XOWN, orow0, pT, pT_free,
                          extra_war=tuple(p5_war))
        pctx.close()
        p0ctx.close()

        f5ctx = contextlib.ExitStack()
        W3A = f5ctx.enter_context(nc.sbuf_tensor(f"W3A_{oc2}", [128, 4096], BF16))
        W3B = f5ctx.enter_context(nc.sbuf_tensor(f"W3B_{oc2}", [128, 4096], BF16))
        GT = f5ctx.enter_context(nc.sbuf_tensor(f"GT_{oc2}", [128, 1024], BF16))

        h128_inline = not (debug and oc2 == 0)
        fw = {}
        hop1_sems = []

        def emit_h128(blk, a_):
            def p6m(t, a_=a_):
                bank = P[4 + a_ % 2]
                tk.wait(t, "pe", upt[("gate", a_)], fw.get(("e128", a_ - 2)))
                ins = t.matmul(bank[:, :t_chunk], H128S[:, :],
                               U[:, a_ * t_chunk:(a_ + 1) * t_chunk],
                               start=True, stop=True)
                fw[("m128", a_)] = tk.inc(ins, "pe")
            blk.tensor(p6m)

            def p6e(v, a_=a_):
                bank = P[4 + a_ % 2]
                tk.wait(v, DVE, tk.tok("dbg"), fw[("m128", a_)])
                tk.wait(v, DVE, tk.tok("dv"))
                fw[("e128", a_)] = tk.inc(v.tensor_copy(
                    U[:, a_ * t_chunk:(a_ + 1) * t_chunk], bank[:, :t_chunk]), "dv")
            blk.vector(p6e)

        def emit_hop1_piece(blk, g):
            a0 = g * 8
            if g % 2 == 0:
                def p6h1s(sp, a0=a0):
                    tk.wait(sp, "sp", fw[("e128", a0 + 7)], *s1_free[0])
                    tk.inc(sp.dma_start(
                        bass.AP(S1, a0 * t_chunk,
                                [[64 * t_chunk, 128], [t_chunk, 8], [1, t_chunk]]),
                        bass.AP(U, a0 * t_chunk,
                                [[NA * t_chunk, 128], [t_chunk, 8], [1, t_chunk]]),
                    ), "dh", 16)
                blk.sync(p6h1s)
                hop1_sems.append(tk.tok("dh"))
            else:
                def p6h1a(a, a0=a0):
                    tk.wait(a, ACT, fw[("e128", a0 + 7)], *s1_free[0])
                    tk.inc(a.dma_start(
                        bass.AP(S1, a0 * t_chunk,
                                [[64 * t_chunk, 128], [t_chunk, 8], [1, t_chunk]]),
                        bass.AP(U, a0 * t_chunk,
                                [[NA * t_chunk, 128], [t_chunk, 8], [1, t_chunk]]),
                    ), "dh3", 16)
                blk.scalar(p6h1a)
                hop1_sems.append(tk.tok("dh3"))

        with BLK() as blk:
            upt = {}
            for gb in range(32):
                b1, s1_ = (WA, "dwa") if gb % 2 == 0 else (WB, "dwb")
                b3, s3_ = (W3A, "dwc") if gb % 2 == 0 else (W3B, "dwd")

                def p5g(g, gb=gb, b1=b1, s1_=s1_, b3=b3, s3_=s3_):
                    tk.wait(g, "pl", tk.tok("dbg"), *out_slot_tok, *xres_slot_tok)
                    tk.wait(g, "pl", xb_cons_dv[0], xb_cons_dv[1],
                            xb_cons_ac[0], xb_cons_ac[1])
                    tk.wait(g, "pl", wcons[s1_], tk.tok("dwe"))
                    tk.inc(g.dma_start(
                        bass.AP(b1, 0, [[8192, 128], [1, 4096]]),
                        bass.AP(W1Bd, gb * 128 * 4096, [[4096, 128], [1, 4096]]),
                    ), s1_, 16)
                    upt[("w1", gb)] = tk.tok(s1_)
                    tk.wait(g, "pl", wcons[s3_])
                    tk.inc(g.dma_start(
                        bass.AP(b3, 0, [[4096, 128], [1, 4096]]),
                        bass.AP(W3Bd, gb * 128 * 4096, [[4096, 128], [1, 4096]]),
                    ), s3_, 16)
                    upt[("w3", gb)] = tk.tok(s3_)
                    emit_casts(g, kmax=1)
                blk.gpsimd(p5g)

                for al in range(2):
                    a_ = gb * 2 + al
                    pu, pl_ = (P[0], P[1]) if a_ % 2 == 0 else (P[2], P[3])

                    def p5t(t, gb=gb, al=al, a_=a_, b1=b1, b3=b3, pu=pu, pl_=pl_):
                        tk.wait(t, "pe", upt[("w1", gb)], upt[("w3", gb)],
                                pT_free[0], pT_free[1], *p5_war)
                        if ("gate", a_ - 2) in upt:
                            tk.wait(t, "pe", upt[("gate", a_ - 2)])
                        last = None
                        for k in range(NDT):
                            last = t.matmul(pu[:, :t_chunk],
                                            b1[:, k * 256 + al * 128: k * 256 + (al + 1) * 128],
                                            XNT[:, k * t_chunk:(k + 1) * t_chunk],
                                            start=(k == 0), stop=(k == NDT - 1))
                        for k in range(NDT):
                            last = t.matmul(pl_[:, :t_chunk],
                                            b3[:, k * 256 + al * 128: k * 256 + (al + 1) * 128],
                                            XNT[:, k * t_chunk:(k + 1) * t_chunk],
                                            start=(k == 0), stop=(k == NDT - 1))
                        upt[("mm", a_)] = tk.inc(last, "pe")
                        if al == 1:
                            wcons["dwa" if gb % 2 == 0 else "dwb"] = tk.tok("pe")
                            wcons["dwc" if gb % 2 == 0 else "dwd"] = tk.tok("pe")
                    blk.tensor(p5t)

                    def p5a(a, a_=a_, pu=pu):
                        tk.wait(a, ACT, upt[("mm", a_)])
                        tk.wait(a, ACT, tk.tok("ac"))
                        upt[("silu", a_)] = tk.inc(a.activation(
                            GT[:, (a_ % 2) * 512:(a_ % 2) * 512 + t_chunk],
                            pu[:, :t_chunk], AF.Silu), "ac")
                    blk.scalar(p5a)

                    def p5v(v, a_=a_, pl_=pl_):
                        tk.wait(v, DVE, upt[("silu", a_)])
                        tk.wait(v, DVE, tk.tok("dv"))
                        upt[("gate", a_)] = tk.inc(v.tensor_tensor(
                            U[:, a_ * t_chunk:(a_ + 1) * t_chunk],
                            GT[:, (a_ % 2) * 512:(a_ % 2) * 512 + t_chunk],
                            pl_[:, :t_chunk], op=ALU.mult), "dv")
                    blk.vector(p5v)

                if h128_inline and gb > 0:
                    for a_ in (2 * gb - 2, 2 * gb - 1):
                        emit_h128(blk, a_)
                    if gb % 4 == 0:
                        emit_hop1_piece(blk, gb // 4 - 1)
            if h128_inline:
                for a_ in (NA - 2, NA - 1):
                    emit_h128(blk, a_)
                emit_hop1_piece(blk, 7)

        xnt_cons[0] = tk.tok("pe")

        if debug and oc2 == 0:
            with BLK() as blk:
                def pdbgu(g):
                    tk.wait(g, "pl", upt[("gate", NA - 1)])
                    tk.inc(g.dma_start(dbg["DBG_U"].ap(), U[:, :]), "dbg", 16)
                blk.gpsimd(pdbgu)

        with BLK() as blk:
            if not h128_inline:
                for a_ in range(NA):
                    emit_h128(blk, a_)
                for g in range(8):
                    emit_hop1_piece(blk, g)
            h1 = (tk.tok("dh"), tk.tok("dh3"))

            def h2piece(th, ch):
                dst = bass.AP(U, th * 64 * (NA * t_chunk) + ch * 64 * TH,
                              [[NA * t_chunk, 64], [TH, 64], [1, TH]])
                srcp = bass.AP(S1, th * TH + ch * 64 * (64 * t_chunk),
                               [[t_chunk, 64], [64 * t_chunk, 64], [1, TH]])
                return dst, srcp

            def p6h2s(sp):
                tk.wait(sp, "sp", *h1)
                for (th, ch), sem in (((0, 0), "dh"), ((1, 0), "dh4")):
                    dst, srcp = h2piece(th, ch)
                    tk.inc(sp.dma_start(dst, srcp), sem, 16)
                fw["hop2a"] = (tk.tok("dh"), tk.tok("dh4"))
            blk.sync(p6h2s)

            def p6h2a(a):
                tk.wait(a, ACT, *h1)
                for (th, ch), sem in (((0, 1), "dh3"), ((1, 1), "dh5")):
                    dst, srcp = h2piece(th, ch)
                    tk.inc(a.dma_start(dst, srcp), sem, 16)
                fw["hop2b"] = (tk.tok("dh3"), tk.tok("dh5"))
            blk.scalar(p6h2a)
            s1_free[0] = fw["hop2a"] + fw["hop2b"]

            nh64 = (128 * TH) // 512
            for ch in range(nh64):
                def p6m2(t, ch=ch):
                    bank = P[ch % 4]
                    tk.wait(t, "pe", *(fw["hop2a"] if ch < nh64 // 2 else fw["hop2b"]))
                    if ("e64", ch - 4) in fw:
                        tk.wait(t, "pe", fw[("e64", ch - 4)])
                    ins = t.matmul(bank[:, :], BLK64[:, :], U[:, ch * 512:(ch + 1) * 512],
                                   start=True, stop=True)
                    fw[("m64", ch)] = tk.inc(ins, "pe")
                blk.tensor(p6m2)

                def p6e2(v, ch=ch):
                    bank = P[ch % 4]
                    tk.wait(v, DVE, fw[("m64", ch)])
                    tk.wait(v, DVE, tk.tok("dv"))
                    fw[("e64", ch)] = tk.inc(v.tensor_copy(
                        U[:, ch * 512:(ch + 1) * 512], bank[:, :]), "dv")
                blk.vector(p6e2)

            def h3piece(th, ch):
                dst = bass.AP(S2, th * TH + ch * 64 * t_chunk,
                              [[128 * t_chunk, 64], [t_chunk, 64], [1, TH]])
                srcp = bass.AP(U, th * 64 * (NA * t_chunk) + ch * 64 * TH,
                               [[NA * t_chunk, 64], [TH, 64], [1, TH]])
                return dst, srcp

            def p6h3s(sp):
                tk.wait(sp, "sp", fw[("e64", nh64 // 2 - 1)], *s2_free[0])
                for th, ch in ((0, 0), (1, 0)):
                    dst, srcp = h3piece(th, ch)
                    tk.inc(sp.dma_start(dst, srcp), "dh", 16)
            blk.sync(p6h3s)

            def p6h3a(a):
                tk.wait(a, ACT, fw[("e64", nh64 - 1)], *s2_free[0])
                for th, ch in ((0, 1), (1, 1)):
                    dst, srcp = h3piece(th, ch)
                    tk.inc(a.dma_start(dst, srcp), "dh3", 16)
            blk.scalar(p6h3a)
            h3 = (tk.tok("dh"), tk.tok("dh3"))

            def h4piece(gg):
                dst = bass.AP(U, gg * 8 * t_chunk,
                              [[NA * t_chunk, 128], [t_chunk, 8], [1, t_chunk]])
                srcp = bass.AP(S2, gg * 8 * 128 * t_chunk,
                               [[t_chunk, 128], [128 * t_chunk, 8], [1, t_chunk]])
                return dst, srcp

            hop4_p = [None] * 8

            def p6h4s(sp):
                tk.wait(sp, "sp", *h3)
                for gg in range(0, 8, 2):
                    dst, srcp = h4piece(gg)
                    tk.inc(sp.dma_start(dst, srcp), "dh", 16)
                    hop4_p[gg] = tk.tok("dh")
            blk.sync(p6h4s)

            def p6h4a(a):
                tk.wait(a, ACT, *h3)
                for gg in range(1, 8, 2):
                    dst, srcp = h4piece(gg)
                    tk.inc(a.dma_start(dst, srcp), "dh3", 16)
                    hop4_p[gg] = tk.tok("dh3")
            blk.scalar(p6h4a)
            fw["hop4"] = (hop4_p[6], hop4_p[7])
            s2_free[0] = fw["hop4"]

        if debug and oc2 == 0:
            with BLK() as blk:
                def p6d(g):
                    tk.wait(g, "pl", *fw["hop4"])
                    tk.inc(g.dma_start(dbg["DBG_FB"].ap(), U[:, :]), "dbg", 16)
                blk.gpsimd(p6d)

        f5ctx.close()
        f7ctx = contextlib.ExitStack()
        XRES = [f7ctx.enter_context(nc.sbuf_tensor(f"XRES{i}_{oc2}", [128, 1024], F32))
                for i in range(NT)]
        OUTB = [f7ctx.enter_context(nc.sbuf_tensor(f"OUTB{i}_{oc2}", [128, 1024], F32))
                for i in range(NT)]
        p67 = contextlib.ExitStack()
        P6b = p67.enter_context(nc.psum_tensor(f"p6_{oc2}", [128, 512], F32))
        P7b = p67.enter_context(nc.psum_tensor(f"p7_{oc2}", [128, 512], F32))
        BANKS8 = list(P[:6]) + [P6b, P7b]

        with BLK() as blk:
            dn = {}
            hop4_tok = fw["hop4"]

            for dh_ in range(2):
                def p7x(sp, dh_=dh_):
                    tk.wait(sp, "sp", tk.tok("dbg"))
                    for ts in range(NT):
                        sem = f"xr{ts}"
                        tk.wait(sp, "sp", xres_slot_tok[ts])
                        tk.inc(sp.dma_start(
                            XRES[ts][:, :],
                            bass.AP(XOWN, (orow0 + ts * 128) * DM + dh_ * 1024,
                                    [[DM, 128], [1, 1024]]),
                        ), sem, 16)
                        dn[("xr", dh_, ts)] = tk.tok(sem)
                blk.sync(p7x)

                for gb in range(16):
                    buf, sem = (WA, "dwa") if gb % 2 == 0 else (WB, "dwb")

                    def p7g(g, dh_=dh_, gb=gb, buf=buf, sem=sem):
                        tk.wait(g, "pl", wcons[sem], tk.tok("dwe"))
                        tk.inc(g.dma_start(
                            bass.AP(buf, 0, [[8192, 128], [1, 4096]]),
                            bass.AP(W2Bd, (dh_ * 16 + gb) * 128 * 4096,
                                    [[4096, 128], [1, 4096]]),
                        ), sem, 16)
                        dn[("w", dh_, gb)] = tk.tok(sem)
                    blk.gpsimd(p7g)

                    def p7t(t, dh_=dh_, gb=gb, buf=buf, sem=sem):
                        tk.wait(t, "pe", *fw["hop4"])
                        tk.wait(t, "pe", dn[("w", dh_, gb)])
                        if gb == 0 and dh_ == 1:
                            tk.wait(t, "pe", dn[("silu", 0)])
                        last = None
                        for al in range(4):
                            a_ = gb * 4 + al
                            for ts in range(NT):
                                for dc in range(2):
                                    bank = BANKS8[(ts * 2 + dc) % len(BANKS8)]
                                    last = t.matmul(
                                        bank[:, :],
                                        U[:, a_ * t_chunk + ts * 128: a_ * t_chunk + (ts + 1) * 128],
                                        buf[:, al * 1024 + dc * 512: al * 1024 + (dc + 1) * 512],
                                        start=(a_ == 0), stop=(a_ == NA - 1))
                        gtok = tk.inc(last, "pe")
                        if gb == 15:
                            dn[("mm", dh_)] = gtok
                        wcons[sem] = gtok
                    blk.tensor(p7t)

                def p7s(a, dh_=dh_):
                    tk.wait(a, ACT, dn[("mm", dh_)], dn.get(("vdone", dh_ - 1)))
                    for i in range(NT * 2):
                        tk.wait(a, ACT, tk.tok("ac"))
                        tk.inc(a.activation(GT8[:, i * 512:(i + 1) * 512],
                                            BANKS8[i % len(BANKS8)][:, :], AF.Silu), "ac")
                    dn[("silu", dh_)] = tk.tok("ac")
                blk.scalar(p7s)

                def p7v(v, dh_=dh_, oc2=oc2):
                    tk.wait(v, DVE, dn[("silu", dh_)])
                    for ts in range(NT):
                        ob, xr = OUTB[ts], XRES[ts]
                        tk.wait(v, DVE, dn[("xr", dh_, ts)])
                        tk.wait(v, DVE, out_slot_tok[ts])
                        for dc in range(2):
                            i = ts * 2 + dc
                            r = oc2 * NT + ts
                            ssl = bass.AP(SBf, r * DM + dh_ * 1024 + dc * 512,
                                          [[NSUB * DM, 128], [1, 512]])
                            tk.wait(v, DVE, tk.tok("dv"))
                            tk.inc(v.tensor_tensor(TMPR[:, :], ssl,
                                                   GT8[:, i * 512:(i + 1) * 512],
                                                   op=ALU.mult), "dv")
                            tk.wait(v, DVE, tk.tok("dv"))
                            tk.inc(v.tensor_tensor(ob[:, dc * 512:(dc + 1) * 512],
                                                   xr[:, dc * 512:(dc + 1) * 512],
                                                   TMPR[:, :], op=ALU.add), "dv")
                        dn[("out", dh_, ts)] = tk.tok("dv")
                        xres_slot_tok[ts] = tk.tok("dv")
                    dn[("vdone", dh_)] = tk.tok("dv")
                blk.vector(p7v)

                def p7o(sp, dh_=dh_):
                    for ts in range(NT):
                        sem = f"do{ts}"
                        tk.wait(sp, "sp", dn[("out", dh_, ts)])
                        tk.inc(sp.dma_start(
                            bass.AP(OUT, (orow0 + ts * 128) * DM + dh_ * 1024,
                                    [[DM, 128], [1, 1024]]),
                            OUTB[ts][:, :]), sem, 16)
                        out_slot_tok[ts] = tk.tok(sem)
                blk.sync(p7o)

        ffn_prev = [tk.tok("dv"), tk.tok("ac")]
        p5_war = [dn[("silu", 1)]]
        p67.close()
        f7ctx.close()

    fctx0.close()

    with BLK() as blk:
        def fin(sp):
            tk.wait(sp, "sp", tk.tok("do0"), tk.tok("do1"), tk.tok("do2"), tk.tok("do3"),
                    tk.tok("dh"), tk.tok("dh3"), tk.tok("dh4"), tk.tok("dh5"),
                    tk.tok("dwe"))
            if debug:
                tk.wait(sp, "sp", tk.tok("dbg"))
        blk.sync(fin)

    _mega.__exit__(None, None, None)
    ctx.close()
    nc.finalize()
    return nc



def _tile_dT(wt, ncols):
    d, o = wt.shape
    a = wt.reshape(NDT, 128, o // ncols, ncols)
    a = a.transpose(2, 1, 0, 3).reshape(o // ncols, 128, NDT * ncols)
    return np.ascontiguousarray(a)


def _tile_w2(w2t):
    a = w2t.reshape(16, 4, 128, 2, 1024)
    a = a.transpose(3, 0, 2, 1, 4).reshape(32, 128, 4096)
    return np.ascontiguousarray(a)


def _host_common(inputs):
    f32 = lambda a: np.ascontiguousarray(np.asarray(a, np.float32))
    return {
        "WKT": _tile_dT(np.asarray(inputs["L0_attn_Wk"], np.float32).T, 512),
        "WVT": _tile_dT(np.asarray(inputs["L0_attn_Wv"], np.float32).T, 512),
        "WOT": _tile_dT(np.asarray(inputs["L0_attn_Wo"], np.float32).T, 512),
        "W1T": _tile_dT(np.asarray(inputs["L0_ffn_W1"], np.float32).T, 256),
        "W3T": _tile_dT(np.asarray(inputs["L0_ffn_W3"], np.float32).T, 256),
        "W2T": _tile_w2(np.asarray(inputs["L0_ffn_W2"], np.float32).T),
        "BK": f32(inputs["L0_attn_bk"]).reshape(1, -1),
        "BV": f32(inputs["L0_attn_bv"]).reshape(1, -1),
        "BO": f32(inputs["L0_attn_bo"]).reshape(1, -1),
        "RT": f32(np.asarray(inputs["L0_attn_rand_matrix"], np.float32).T),
        "IDENT": np.eye(128, dtype=np.float32),
        "ONESR": np.ones((1, 128), np.float32),
        "H128S": (_hadamard(128) / 16.0).astype(np.float32),
        "BLK64": np.kron(np.eye(2), _hadamard(64) * (np.sqrt(2.0) / 8.0)).astype(np.float32),
    }


def _rope_tables(positions):
    inv_freq = 1.0 / (10000.0 ** (np.arange(0, DH, 2, dtype=np.float64) / DH))
    freqs = positions[:, None].astype(np.float64) * inv_freq[None, :]
    cos = np.cos(freqs).astype(np.float32)
    sin = np.sin(freqs).astype(np.float32)
    cos_full = np.ascontiguousarray(np.concatenate([cos, cos], axis=1))
    sin_signed = np.ascontiguousarray(np.concatenate([-sin, sin], axis=1))
    return cos_full, sin_signed


def kernel(**inputs):
    X = np.ascontiguousarray(np.asarray(inputs["L0_attn_hidden_states"], np.float32)
                             ).reshape(8192, DM)
    common = _host_common(inputs)
    WqT = np.ascontiguousarray(np.asarray(inputs["L0_attn_Wq"], np.float32).T)
    bq = np.asarray(inputs["L0_attn_bq"], np.float32)
    cosf, sins = _rope_tables(np.arange(TBATCH))
    common["COS"] = cosf
    common["SINS"] = sins
    nc = build_program(debug=False)
    in_maps = []
    for core in range(NCORES):
        b, par = core // 2, core % 2
        m = dict(common)
        hbase = par * HQL * DH
        m["WQTS"] = _tile_dT(WqT[:, hbase:hbase + HQL * DH], 512)
        m["BQ"] = np.ascontiguousarray(bq[hbase:hbase + HQL * DH]).reshape(1, -1)
        m["XBAT"] = np.ascontiguousarray(X[b * TBATCH:(b + 1) * TBATCH])
        m["XOWN"] = np.ascontiguousarray(
            X[b * TBATCH + par * TOWN: b * TBATCH + (par + 1) * TOWN])
        in_maps.append(m)
    res = run_bass_kernel_spmd(nc, in_maps, core_ids=list(range(NCORES)))
    out = np.concatenate([np.asarray(r["OUT"], np.float32) for r in res.results], axis=0)
    return out.reshape(4, 2048, DM)


# version 18
# speedup vs baseline: 1.0462x; 1.0462x over previous
import contextlib

import numpy as np
import ml_dtypes

import concourse.bass as bass
import concourse.mybir as mybir
from concourse.bass_utils import run_bass_kernel_spmd

F32 = mybir.dt.float32
BF16 = mybir.dt.bfloat16
AF = mybir.ActivationFunctionType
ALU = mybir.AluOpType
AX = mybir.AxisListType
PI = float(np.pi)

NCORES = 8
DM = 2048
NDT = DM // 128
HQL = 8
HKV, DH = 8, 128
FF = 8192
NA = FF // 128
EPS_LN = 1e-5
TBATCH = 2048
TOWN = 1024


def _hadamard(n):
    h = np.array([[1.0]])
    while h.shape[0] < n:
        h = np.block([[h, h], [h, -h]])
    return h


class TK:

    def __init__(self, nc, ctx, names):
        self.h = {n: ctx.enter_context(nc.semaphore(n)) for n in names}
        self.cnt = {n: 0 for n in names}
        self.waited = {}

    def inc(self, instr, name, n=1):
        self.cnt[name] += n
        instr.then_inc(self.h[name], n)
        return (name, self.cnt[name])

    def tok(self, name):
        return (name, self.cnt[name])

    def wait(self, eng, ek, *toks):
        for t in toks:
            if t is None:
                continue
            name, v = t
            if v > 0 and self.waited.get((ek, name), 0) < v:
                eng.wait_ge(self.h[name], v)
                self.waited[(ek, name)] = v


def build_program(t_own=TOWN, t_batch=TBATCH, t_chunk=512, debug=False):
    nc = bass.Bass("TRN2", target_bir_lowering=False)
    n_ac = t_batch // t_chunk
    n_oc = t_own // t_chunk
    NT = t_chunk // 128
    TH = t_chunk // 2
    NSUB = t_own // 128

    XBAT = nc.dram_tensor("XBAT", [t_batch, DM], F32, kind="ExternalInput")
    XOWN = nc.dram_tensor("XOWN", [t_own, DM], F32, kind="ExternalInput")
    OUT = nc.dram_tensor("OUT", [t_own, DM], F32, kind="ExternalOutput")
    WQTS = nc.dram_tensor("WQTS", [2, 128, NDT * 512], F32, kind="ExternalInput")
    WKT = nc.dram_tensor("WKT", [2, 128, NDT * 512], F32, kind="ExternalInput")
    WVT = nc.dram_tensor("WVT", [2, 128, NDT * 512], F32, kind="ExternalInput")
    WOT = nc.dram_tensor("WOT", [4, 128, NDT * 512], F32, kind="ExternalInput")
    W1T = nc.dram_tensor("W1T", [32, 128, NDT * 256], F32, kind="ExternalInput")
    W3T = nc.dram_tensor("W3T", [32, 128, NDT * 256], F32, kind="ExternalInput")
    W2T = nc.dram_tensor("W2T", [32, 128, 4 * 1024], F32, kind="ExternalInput")
    BQ = nc.dram_tensor("BQ", [1, HQL * DH], F32, kind="ExternalInput")
    BK = nc.dram_tensor("BK", [1, HKV * DH], F32, kind="ExternalInput")
    BV = nc.dram_tensor("BV", [1, HKV * DH], F32, kind="ExternalInput")
    BO = nc.dram_tensor("BO", [1, DM], F32, kind="ExternalInput")
    RTD = nc.dram_tensor("RT", [DH, 64], F32, kind="ExternalInput")
    IDENTD = nc.dram_tensor("IDENT", [128, 128], F32, kind="ExternalInput")
    ONESD = nc.dram_tensor("ONESR", [1, 128], F32, kind="ExternalInput")
    H128D = nc.dram_tensor("H128S", [128, 128], F32, kind="ExternalInput")
    BLKD = nc.dram_tensor("BLK64", [128, 128], F32, kind="ExternalInput")
    COSD = nc.dram_tensor("COS", [t_batch, 128], F32, kind="ExternalInput")
    SIND = nc.dram_tensor("SINS", [t_batch, 128], F32, kind="ExternalInput")
    S1 = nc.dram_tensor("S1", [FF, t_chunk], BF16)
    S2 = nc.dram_tensor("S2", [FF, t_chunk], BF16)
    W1Bd = nc.dram_tensor("W1Bd", [32, 128, NDT * 256], BF16)
    W3Bd = nc.dram_tensor("W3Bd", [32, 128, NDT * 256], BF16)
    W2Bd = nc.dram_tensor("W2Bd", [32, 128, 4 * 1024], BF16)

    dbg = {}
    if debug:
        for name, shape in [
            ("DBG_QP", [128, NT * HQL * DH]), ("DBG_KP", [128, NT * HKV * DH]),
            ("DBG_A", [128, 64]), ("DBG_MIXT", [128, HQL * t_batch]),
            ("DBG_S", [128, NSUB * DM]), ("DBG_U", [128, NA * t_chunk]),
            ("DBG_FB", [128, NA * t_chunk]), ("DBG_QROT", [128, NT * HQL * DH]),
            ("DBG_P0Q", [128, HQL * 64]),
        ]:
            dbg[name] = nc.dram_tensor(name, shape, BF16, kind="ExternalOutput")

    ctx = contextlib.ExitStack()
    sb = lambda nm, shp, dt: ctx.enter_context(nc.sbuf_tensor(nm, shp, dt))

    tk = TK(nc, ctx, [
        "dxa", "dxb", "dwa", "dwb", "dwc", "dwd", "dcst", "dh", "dh2", "dh3",
        "dh4", "dh5", "dwe", "dbg",
        "do0", "do1", "do2", "do3", "xr0", "xr1", "xr2", "xr3", "pe", "dv", "ac", "po",
    ])

    IDENT = sb("IDENTs", [128, 128], BF16)
    ONES = sb("ONESs", [1, 128], BF16)
    H128S = sb("H128Sb", [128, 128], BF16)
    BLK64 = sb("BLK64b", [128, 128], BF16)
    RT16 = sb("RT16", [DH, 64], BF16)
    COSB = sb("COSB", [128, t_batch], BF16)
    SINSB = sb("SINSB", [128, t_batch], BF16)
    BQs = sb("BQs", [1, HQL * DH], BF16)
    BKs = sb("BKs", [1, HKV * DH], BF16)
    BVs = sb("BVs", [1, HKV * DH], BF16)
    BOs = sb("BOs", [1, DM], BF16)
    XNT = sb("XNT", [128, NDT * t_chunk], BF16)
    SBf = sb("SBf", [128, NSUB * DM], BF16)
    WA = sb("WA", [128, 8192], BF16)
    WB = sb("WB", [128, 8192], BF16)
    ROWS = sb("ROWS", [128, 160], F32)

    P = [ctx.enter_context(nc.psum_tensor(f"p{i}", [128, 512], F32)) for i in range(6)]

    DVE, ACT = "dv", "ac"

    def rcol(ts, idx, n=1):
        return ROWS[:, ts * 24 + idx: ts * 24 + idx + n]

    C_SUM, C_SUMSQ, C_MEAN, C_M2, C_VAR, C_SV, C_RSTD = 0, 1, 2, 3, 4, 5, 6
    C_NQ = 8

    with nc.Block() as blk:
        def cst(g):
            for dst, src in [
                (IDENT[:, :], IDENTD.ap()), (ONES[:, :], ONESD.ap()),
                (H128S[:, :], H128D.ap()), (BLK64[:, :], BLKD.ap()),
                (RT16[:, :], RTD.ap()), (BQs[:, :], BQ.ap()),
                (BKs[:, :], BK.ap()), (BVs[:, :], BV.ap()), (BOs[:, :], BO.ap()),
            ]:
                tk.inc(g.dma_start(dst, src), "dcst", 16)
            nslot = t_batch // 128
            for dstt, srct in [(COSB, COSD), (SINSB, SIND)]:
                tk.inc(g.dma_start(
                    bass.AP(dstt, 0, [[t_batch, 128], [128, nslot], [1, 128]]),
                    bass.AP(srct, 0, [[128, 128], [128 * 128, nslot], [1, 128]]),
                ), "dcst", 16)
        blk.gpsimd(cst)
    cst_tok = tk.tok("dcst")

    class _SameBlk:
        def __init__(self, b):
            self.b = b

        def __enter__(self):
            return self.b

        def __exit__(self, *a):
            return False

    _mega = nc.Block()
    _megah = _mega.__enter__()
    BLK = lambda: _SameBlk(_megah)

    wcons = {"dwa": None, "dwb": None, "dwc": None, "dwd": None}
    pool_prev = [None]
    out_slot_tok = [None] * 4
    xres_slot_tok = [None] * 4
    xb_cons_dv = [None, None]
    xb_cons_ac = [None, None]
    xnt_cons = [None]

    mixt_ctx = contextlib.ExitStack()
    MIXT = mixt_ctx.enter_context(nc.sbuf_tensor("MIXT", [128, HQL * t_batch], BF16))

    bufs = {}

    def emit_ln_chunk(blk, xsrc, row0, pT, pT_free, extra_war=()):
        XBa, XBb = bufs["XBa"], bufs["XBb"]
        XNa, XNb = bufs["XNa"], bufs["XNb"]
        for ts in range(NT):
            xn = XNa if ts % 2 == 0 else XNb
            xb = XBa if ts % 2 == 0 else XBb
            dxs = "dxa" if ts % 2 == 0 else "dxb"
            st = {}

            def p0ld(sp, ts=ts, xb=xb, dxs=dxs):
                tk.wait(sp, "sp", tk.tok("dbg"), *extra_war)
                tk.wait(sp, "sp", *out_slot_tok, *xres_slot_tok)
                tk.wait(sp, "sp", xb_cons_dv[ts % 2], xb_cons_ac[ts % 2])
                tk.inc(sp.dma_start(
                    xb[:, :],
                    bass.AP(xsrc, (row0 + ts * 128) * DM, [[DM, 128], [1, DM]]),
                ), dxs, 16)
                st["x"] = tk.tok(dxs)
            blk.sync(p0ld)
            xsl = xb[:, :]

            def p0a(a, ts=ts, xsl=xsl, xn=xn):
                tk.wait(a, ACT, st["x"], cst_tok, *extra_war)
                tk.wait(a, ACT, xb_cons_dv[ts % 2], xb_cons_ac[ts % 2])
                st["sq"] = tk.inc(a.activation(xn[:, :], xsl, AF.Square,
                                               accum_out=rcol(ts, C_SUMSQ)), "ac")
            blk.scalar(p0a)

            def p0v(v, ts=ts, xsl=xsl):
                tk.wait(v, DVE, st["x"], *extra_war)
                tk.wait(v, DVE, tk.tok("dv"))
                tk.inc(v.tensor_reduce(rcol(ts, C_SUM), xsl, axis=AX.X, op=ALU.add), "dv")
                tk.wait(v, DVE, tk.tok("dv"))
                tk.inc(v.tensor_scalar_mul(rcol(ts, C_MEAN), rcol(ts, C_SUM), 1.0 / DM), "dv")
                tk.wait(v, DVE, tk.tok("dv"))
                tk.inc(v.tensor_mul(rcol(ts, C_M2), rcol(ts, C_MEAN), rcol(ts, C_MEAN)), "dv")
                tk.wait(v, DVE, st["sq"])
                tk.wait(v, DVE, tk.tok("dv"))
                tk.inc(v.tensor_scalar(rcol(ts, C_VAR), rcol(ts, C_SUMSQ),
                                       1.0 / DM, EPS_LN, op0=ALU.mult, op1=ALU.add), "dv")
                tk.wait(v, DVE, tk.tok("dv"))
                st["var"] = tk.inc(v.tensor_sub(
                    rcol(ts, C_VAR), rcol(ts, C_VAR), rcol(ts, C_M2)), "dv")
            blk.vector(p0v)

            def p0a2(a, ts=ts):
                tk.wait(a, ACT, st["var"])
                st["sv"] = tk.inc(a.activation(rcol(ts, C_SV), rcol(ts, C_VAR), AF.Sqrt), "ac")
            blk.scalar(p0a2)

            def p0v2(v, ts=ts, xn=xn, xsl=xsl):
                tk.wait(v, DVE, st["sv"], xnt_cons[0])
                tk.wait(v, DVE, tk.tok("dv"))
                tk.inc(v.reciprocal(rcol(ts, C_RSTD), rcol(ts, C_SV)), "dv")
                tk.wait(v, DVE, tk.tok("dv"))
                st["xn"] = tk.inc(v.tensor_scalar(
                    xn[:, :], xsl, rcol(ts, C_MEAN), rcol(ts, C_RSTD),
                    op0=ALU.subtract, op1=ALU.mult), "dv")
                xb_cons_dv[ts % 2] = st["xn"]
                xb_cons_ac[ts % 2] = st["sq"]
            blk.vector(p0v2)

            for half in range(2):
                bank = pT[half % len(pT)]
                bid = half % len(pT)

                def p0t(t, ts=ts, half=half, xn=xn, bank=bank, bid=bid):
                    tk.wait(t, "pe", st["xn"], cst_tok, pT_free[bid])
                    ins = None
                    for k in range(8):
                        kk = half * 8 + k
                        ins = t.transpose(bank[:, k * 128:(k + 1) * 128],
                                          xn[:, kk * 128:(kk + 1) * 128], IDENT[:, :])
                    st[f"tr{half}"] = tk.inc(ins, "pe")
                blk.tensor(p0t)

                def p0ea(a, ts=ts, half=half, bank=bank, bid=bid):
                    tk.wait(a, ACT, st[f"tr{half}"], xnt_cons[0])
                    pT_free[bid] = tk.inc(a.copy(
                        bass.AP(XNT, half * 8 * t_chunk + ts * 128,
                                [[NDT * t_chunk, 128], [t_chunk, 8], [1, 128]]),
                        bass.AP(bank, 0, [[1024, 128], [128, 8], [1, 128]])), "ac")
                blk.scalar(p0ea)

    actx0 = contextlib.ExitStack()
    asb0 = lambda nm, shp, dt: actx0.enter_context(nc.sbuf_tensor(nm, shp, dt))
    bufs["XBa"] = asb0("XBa", [128, DM], F32)
    bufs["XBb"] = asb0("XBb", [128, DM], F32)
    bufs["XNa"] = asb0("XNa", [128, DM], BF16)
    bufs["XNb"] = asb0("XNb", [128, DM], BF16)
    QB = asb0("QB", [128, NT * HQL * DH], BF16)
    KB = asb0("KB", [128, NT * HKV * DH], BF16)
    VB = asb0("VB", [128, NT * HKV * DH], BF16)
    TMP1 = asb0("TMP1", [128, HQL * DH], BF16)
    TMP2 = asb0("TMP2", [128, HKV * DH], BF16)
    QKT = asb0("QKT", [128, (HQL + HKV) * 128], BF16)
    QP = asb0("QP", [128, NT * HQL * DH], BF16)
    KP = asb0("KP", [128, NT * HKV * DH], BF16)
    ATMP = TMP1
    PSCR = asb0("PSCR", [128, 1024], BF16)
    AB = ROWS[:, 96:160]
    SBW = NSUB * DM
    QT2_OFF, KT2_OFF = 0, 1024
    KROT_OFF = 6144
    P0Q_OFF, P0K_OFF = 2048, 4096
    MIX_OFF = 10240

    pctx = contextlib.ExitStack()
    pT = [pctx.enter_context(nc.psum_tensor(f"pT{i}", [128, 1024], BF16)) for i in range(2)]
    pT_free = {0: None, 1: None}
    fb_free = {0: None, 1: None, 4: None, 5: None}
    p4_free = [None]
    prev_sin = [None]
    sin_slot_prev = [None] * NT
    feat_prev = [None]
    p1_last = [None]
    evts = {}
    stss = {}
    pool_prev = [None]

    p4evt = {}
    p4wbuf = [None, None]

    def emit_p4g(blk, dc, extra=()):
        if dc < 2:
            buf, sem = (WA, "dwa") if dc == 0 else (WB, "dwb")
        else:
            buf, sem = p4wbuf[dc - 2], ("dwc" if dc == 2 else "dwd")

        def p4g(g, dc=dc, buf=buf, sem=sem, extra=extra):
            tk.wait(g, "pl", tk.tok("dbg"), *extra)
            tk.wait(g, "pl", wcons[sem])
            tk.inc(g.dma_start(
                bass.AP(buf, 0, [[8192, 128], [1, 8192]]),
                bass.AP(WOT, dc * 128 * 8192, [[8192, 128], [1, 8192]]),
            ), sem, 16)
            p4evt[("w", dc)] = tk.tok(sem)
        blk.gpsimd(p4g)

    jobs = ([(WQTS, BQs, QB, oc) for oc in range(2)]
            + [(WKT, BKs, KB, oc) for oc in range(2)]
            + [(WVT, BVs, VB, oc) for oc in range(2)])

    cast_list = [(wd, ws, j)
                 for ws, wd in ((W1T, W1Bd), (W3T, W3Bd), (W2T, W2Bd))
                 for j in range(32)]
    cast_ptr = [0]

    def emit_casts(g, kmax=6):
        for _ in range(kmax):
            if cast_ptr[0] >= len(cast_list):
                return
            wd, ws, j = cast_list[cast_ptr[0]]
            cast_ptr[0] += 1
            tk.inc(g.dma_start(
                bass.AP(wd, j * 128 * 4096, [[4096, 128], [1, 4096]]),
                bass.AP(ws, j * 128 * 4096, [[4096, 128], [1, 4096]]),
            ), "dwe", 16)

    def emit_p1g(blk, c, ji):
        evt = evts[c]
        wd, bs, ob, oc = jobs[ji]
        buf, sem = (WA, "dwa") if ji % 2 == 0 else (WB, "dwb")

        def p1g(g, buf=buf, sem=sem, wd=wd, oc=oc, ji=ji, evt=evt):
            tk.wait(g, "pl", wcons[sem])
            tk.inc(g.dma_start(
                bass.AP(buf, 0, [[8192, 128], [1, 8192]]),
                bass.AP(wd, oc * 128 * 8192, [[8192, 128], [1, 8192]]),
            ), sem, 16)
            evt[("w", ji)] = tk.tok(sem)
            if c > 0:
                emit_casts(g)
        blk.gpsimd(p1g)

    def emit_p1t(blk, c, ji):
        evt = evts[c]
        wd, bs, ob, oc = jobs[ji]
        buf, sem = (WA, "dwa") if ji % 2 == 0 else (WB, "dwb")

        def p1t(t, ji=ji, buf=buf, bs=bs, oc=oc, sem=sem, evt=evt):
            tk.wait(t, "pe", evt[("w", ji)], pT_free[0], pT_free[1],
                    feat_prev[0], p1_last[0])
            for ts in range(NT):
                bank = P[ts % 4]
                if (ji - 1, ts) in evt:
                    tk.wait(t, "pe", evt[(ji - 1, ts)])
                t.matmul(bank[:, :], ONES[:, :], bs[:, oc * 512:(oc + 1) * 512],
                         start=True, stop=False)
                last = None
                for k in range(NDT):
                    last = t.matmul(
                        bank[:, :],
                        XNT[:, k * t_chunk + ts * 128: k * t_chunk + (ts + 1) * 128],
                        buf[:, k * 512:(k + 1) * 512],
                        start=False, stop=(k == NDT - 1))
                evt[("mm", ji, ts)] = tk.inc(last, "pe")
            wcons[sem] = tk.tok("pe")
        blk.tensor(p1t)
        if ji + 2 < 6:
            emit_p1g(blk, c, ji + 2)

    def p1a_ops(a, c, ji):
        evt = evts[c]
        prev_sts = stss.get(c - 1)
        wd, bs, ob, oc = jobs[ji]
        tk.wait(a, ACT, tk.tok("dbg"))
        for ts in range(NT):
            bank = P[ts % 4]
            tk.wait(a, ACT, evt[("mm", ji, ts)])
            if prev_sts is not None:
                tk.wait(a, ACT, prev_sts[ts]["tr"], prev_sts[ts]["mx"])
            evt[(ji, ts)] = tk.inc(a.copy(
                bass.AP(ob, ts * 1024 + oc * 512, [[NT * 1024, 128], [1, 512]]),
                bank[:, :]), "ac")

    def emit_p1_main(blk, c, full=False):
        evts[c] = {}
        emit_p1g(blk, c, 0)
        emit_p1g(blk, c, 1)
        emit_p1t(blk, c, 0)
        if full:
            for ji in range(1, 5):
                blk.scalar(lambda a, ji=ji: p1a_ops(a, c, ji - 1))
                emit_p1t(blk, c, ji)

    def emit_p1_tail(blk, c, mixc):
        evt = evts[c]

        def p1av(a, evt=evt, mixc=mixc):
            tk.wait(a, ACT, tk.tok("dbg"))
            for ts in range(NT):
                bank = P[ts % 4]
                tk.wait(a, ACT, evt[("mm", 4, ts)])
                if mixc is not None:
                    tk.wait(a, ACT, *mixc[ts])
                evt[(4, ts)] = tk.inc(a.copy(
                    bass.AP(VB, ts * 1024, [[NT * 1024, 128], [1, 512]]),
                    bank[:, :]), "ac")
        blk.scalar(p1av)

        def p1tv(t, evt=evt):
            tk.wait(t, "pe", evt[("w", 5)])
            for ts in range(NT):
                bank = P[ts % 4]
                tk.wait(t, "pe", evt[(4, ts)])
                t.matmul(bank[:, :], ONES[:, :], BVs[:, 512:1024], start=True, stop=False)
                last = None
                for k in range(NDT):
                    last = t.matmul(
                        bank[:, :],
                        XNT[:, k * t_chunk + ts * 128: k * t_chunk + (ts + 1) * 128],
                        WB[:, k * 512:(k + 1) * 512],
                        start=False, stop=(k == NDT - 1))
                evt[("mm", 5, ts)] = tk.inc(last, "pe")
            wcons["dwb"] = tk.tok("pe")
        blk.tensor(p1tv)

        def p1av2(a, evt=evt):
            for ts in range(NT):
                bank = P[ts % 4]
                tk.wait(a, ACT, evt[("mm", 5, ts)])
                evt[(5, ts)] = tk.inc(a.copy(
                    bass.AP(VB, ts * 1024 + 512, [[NT * 1024, 128], [1, 512]]),
                    bank[:, :]), "ac")
        blk.scalar(p1av2)
        xnt_cons[0] = tk.tok("pe")
        p1_last[0] = evt[(5, NT - 1)]

    for ac in range(n_ac):
        arow0 = ac * t_chunk

        if ac == 0:
            with BLK() as blk:
                emit_ln_chunk(blk, XBAT, 0, [pT[0]], pT_free)
                emit_p1_main(blk, 0, full=True)
                emit_p1_tail(blk, 0, None)
        evt = evts[ac]

        sts2 = stss[ac] = {}
        mix_toks = {}
        a_done = {}
        with BLK() as blk:
            for ts in range(NT):
                slot = ac * NT + ts
                st = sts2[ts] = {}

                def p2ropek(g, ts=ts, slot=slot, st=st, evt=evt):
                    tk.wait(g, "pl", tk.tok("po"))
                    tk.wait(g, "pl", evt.get((2, ts)), evt.get((3, ts)))
                    base = ts * HKV * DH
                    kap = bass.AP(KB, base, [[NT * HKV * DH, 128], [128, HKV], [1, 128]])
                    cosap = bass.AP(COSB, slot * 128, [[t_batch, 128], [0, HKV], [1, 128]])
                    kswap = bass.AP(KB, base + 64,
                                    [[NT * HKV * DH, 128], [128, HKV], [-64, 2], [1, 64]])
                    sinap = bass.AP(SINSB, slot * 128,
                                    [[t_batch, 128], [0, HKV], [64, 2], [1, 64]])
                    t1 = bass.AP(SBf, KROT_OFF, [[SBW, 128], [128, HKV], [1, 128]])
                    t2 = bass.AP(SBf, KROT_OFF + 1024,
                                 [[SBW, 128], [128, HKV], [64, 2], [1, 64]])
                    t2r = bass.AP(SBf, KROT_OFF + 1024, [[SBW, 128], [128, HKV], [1, 128]])
                    tk.inc(g.tensor_tensor(t1, kap, cosap, op=ALU.mult), "po")
                    tk.wait(g, "pl", tk.tok("po"))
                    tk.inc(g.tensor_tensor(t2, kswap, sinap, op=ALU.mult), "po")
                    tk.wait(g, "pl", tk.tok("po"))
                    st["rotk"] = tk.inc(g.tensor_tensor(kap, t1, t2r, op=ALU.add), "po")
                blk.gpsimd(p2ropek)

                def p2rope(v, ts=ts, slot=slot, st=st, evt=evt):
                    tk.wait(v, DVE, tk.tok("dv"))
                    for ji in range(2):
                        tk.wait(v, DVE, evt.get((ji, ts)))
                    base = ts * HQL * DH
                    qap = bass.AP(QB, base, [[NT * HQL * DH, 128], [128, HQL], [1, 128]])
                    cosap = bass.AP(COSB, slot * 128, [[t_batch, 128], [0, HQL], [1, 128]])
                    qswap = bass.AP(QB, base + 64,
                                    [[NT * HQL * DH, 128], [128, HQL], [-64, 2], [1, 64]])
                    sinap = bass.AP(SINSB, slot * 128,
                                    [[t_batch, 128], [0, HQL], [64, 2], [1, 64]])
                    t1 = bass.AP(TMP1, 0, [[HQL * DH, 128], [128, HQL], [1, 128]])
                    t2 = bass.AP(SBf, QT2_OFF, [[SBW, 128], [128, HQL], [64, 2], [1, 64]])
                    t2r = bass.AP(SBf, QT2_OFF, [[SBW, 128], [128, HQL], [1, 128]])
                    v.tensor_tensor(t1, qap, cosap, op=ALU.mult)
                    tk.inc(v.tensor_tensor(t2, qswap, sinap, op=ALU.mult), "dv")
                    tk.wait(v, DVE, tk.tok("dv"))
                    st["rot"] = tk.inc(v.tensor_tensor(qap, t1, t2r, op=ALU.add), "dv")
                    tk.wait(v, DVE, tk.tok("dv"))
                    flatq = bass.AP(QB, base, [[NT * HQL * DH, 128], [1, HQL * DH]])
                    tk.inc(v.tensor_tensor(TMP1[:, :], flatq, flatq, op=ALU.mult), "dv")
                    tk.wait(v, DVE, st["rotk"])
                    baske = ts * HKV * DH
                    flatk = bass.AP(KB, baske, [[NT * HKV * DH, 128], [1, HKV * DH]])
                    tk.inc(v.tensor_tensor(TMP2[:, :], flatk, flatk, op=ALU.mult), "dv")
                    tk.wait(v, DVE, tk.tok("dv"))
                    for coff, tmp in ((C_NQ, TMP1), (C_NQ + HQL, TMP2)):
                        ins = v.tensor_reduce(
                            rcol(ts, coff, HQL),
                            bass.AP(tmp, 0, [[HQL * DH, 128], [128, HQL], [1, 128]]),
                            axis=AX.X, op=ALU.add)
                    tk.inc(ins, "dv")
                    tk.wait(v, DVE, tk.tok("dv"))
                    st["mx"] = tk.inc(v.tensor_scalar_max(
                        rcol(ts, C_NQ, HQL + HKV), rcol(ts, C_NQ, HQL + HKV), 1e-12), "dv")
                blk.vector(p2rope)

                def p2tr(t, ts=ts, st=st):
                    tk.wait(t, "pe", st["rot"], st["rotk"], pT_free[0], pT_free[1])
                    for gi in range(2):
                        bank = pT[gi]
                        for k in range(8):
                            if gi == 0:
                                srcq = bass.AP(QB, ts * HQL * DH + k * 128,
                                               [[NT * HQL * DH, 128], [1, 128]])
                            else:
                                srcq = bass.AP(KB, ts * HKV * DH + k * 128,
                                               [[NT * HKV * DH, 128], [1, 128]])
                            ins = t.transpose(bank[:, k * 128:(k + 1) * 128], srcq,
                                              IDENT[:, :])
                    st["tr"] = tk.inc(ins, "pe")
                blk.tensor(p2tr)

                def p2te(a, ts=ts, st=st):
                    tk.wait(a, ACT, st["tr"], prev_sin[0])
                    if ts > 0:
                        tk.wait(a, ACT, sts2[ts - 1]["fm"])
                    for gi in range(2):
                        ins = a.copy(QKT[:, gi * 1024:(gi + 1) * 1024], pT[gi][:, :])
                    st["te"] = tk.inc(ins, "ac")
                    pT_free[0] = pT_free[1] = st["te"]
                blk.scalar(p2te)

                fbanks = (4, 5) if ts % 2 == 0 else (0, 1)

                def p2fm(t, ts=ts, st=st, fbanks=fbanks, evt=evt):
                    tk.wait(t, "pe", st["te"], fb_free[fbanks[0]], fb_free[fbanks[1]])
                    if ts % 2 == 1:
                        tk.wait(t, "pe", evt.get((5, 0)), evt.get((5, 1)))
                    for gi in range(2):
                        fb = P[fbanks[gi]]
                        for k in range(8):
                            ins = t.matmul(fb[:, k * 64:(k + 1) * 64],
                                           QKT[:, (gi * 8 + k) * 128:(gi * 8 + k + 1) * 128],
                                           RT16[:, :], start=True, stop=True)
                    st["fm"] = tk.inc(ins, "pe")
                blk.tensor(p2fm)

                def p2fe(a, ts=ts, st=st, fbanks=fbanks):
                    tk.wait(a, ACT, st["fm"], sin_slot_prev[ts])
                    for gi in range(2):
                        off = (P0Q_OFF if gi == 0 else P0K_OFF) + ts * 512
                        ins = a.copy(
                            bass.AP(SBf, off, [[SBW, 128], [1, 512]]),
                            P[fbanks[gi]][:, :])
                    st["fe"] = tk.inc(ins, "ac")
                    fb_free[fbanks[0]] = fb_free[fbanks[1]] = st["fe"]
                blk.scalar(p2fe)
            feat_prev[0] = sts2[NT - 1]["fe"]

            if debug and ac == 0:
                def pdbgr(g):
                    tk.wait(g, "pl", sts2[NT - 1]["mx"])
                    tk.inc(g.dma_start(dbg["DBG_QROT"].ap(), QB[:, :]), "dbg", 16)
                blk.gpsimd(pdbgr)

            bt2 = {}

            def p2sqr(a):
                tk.wait(a, ACT, sts2[NT - 1]["mx"])
                nap = bass.AP(ROWS, C_NQ, [[160, 128], [24, NT], [1, HQL + HKV]])
                bt2["sqrt"] = tk.inc(a.activation(nap, nap, AF.Sqrt), "ac")
            blk.scalar(p2sqr)

            def p2inv(v):
                tk.wait(v, DVE, bt2["sqrt"])
                nap = bass.AP(ROWS, C_NQ, [[160, 128], [24, NT], [1, HQL + HKV]])
                bt2["inv"] = tk.inc(v.reciprocal(nap, nap), "dv")
            blk.vector(p2inv)

        if ac + 1 < n_ac:
            with BLK() as blk:
                emit_ln_chunk(blk, XBAT, (ac + 1) * t_chunk, [pT[0]], pT_free)
                emit_p1_main(blk, ac + 1)

        HSPL = 4
        nxt = ac + 1 if ac + 1 < n_ac else None
        pool_mix_tok = {}
        mixdv = {}

        def emit_p3m(blk, ts2):
            def p3m(v, ts2=ts2):
                tk.wait(v, DVE, tk.tok("dv"))
                tk.wait(v, DVE, evt.get((4, ts2)), evt.get((5, ts2)))
                vfull = bass.AP(VB, ts2 * HKV * DH, [[NT * HKV * DH, 128], [1, 1024]])
                for hp in range(0, HSPL, 2):
                    tk.wait(v, DVE, tk.tok("dv"))
                    for hh, tmp in ((hp, TMP1), (hp + 1, TMP2)):
                        abb = bass.AP(ROWS, 96 + hh * 8, [[160, 128], [1, 8], [0, 128]])
                        ins = v.tensor_tensor(tmp[:, :], vfull, abb, op=ALU.mult)
                    tk.inc(ins, "dv")
                    for w in (512, 256):
                        tk.wait(v, DVE, tk.tok("dv"))
                        for tmp in (TMP1, TMP2):
                            ins = v.tensor_tensor(tmp[:, :w], tmp[:, :w],
                                                  tmp[:, w:2 * w], op=ALU.add)
                        tk.inc(ins, "dv")
                    tk.wait(v, DVE, tk.tok("dv"))
                    for hh, tmp in ((hp, TMP1), (hp + 1, TMP2)):
                        msl = bass.AP(SBf, MIX_OFF + ts2 * HQL * DH + hh * 128,
                                      [[SBW, 128], [1, 128]])
                        ins = v.tensor_tensor(msl, tmp[:, 0:128],
                                              tmp[:, 128:256], op=ALU.add)
                    tk.inc(ins, "dv")
            blk.vector(p3m)
            mixdv[ts2] = tk.tok("dv")
            mix_toks[ts2] = (mixdv[ts2], pool_mix_tok[ts2])

            def p4tr(t, ts2=ts2):
                tk.wait(t, "pe", *mix_toks[ts2])
                tk.wait(t, "pe", pT_free[1] if ts2 == 0 else None, p4_free[0])
                ins = None
                for h in range(HQL):
                    ins = t.transpose(
                        pT[1][:, h * 128:(h + 1) * 128],
                        bass.AP(SBf, MIX_OFF + ts2 * HQL * DH + h * 128,
                                [[SBW, 128], [1, 128]]),
                        IDENT[:, :])
                sts2[ts2]["p4tr"] = tk.inc(ins, "pe")
            blk.tensor(p4tr)

        def p4te_ops(a, ts2):
            tk.wait(a, ACT, sts2[ts2]["p4tr"])
            p4_free[0] = tk.inc(a.copy(
                bass.AP(MIXT, arow0 + ts2 * 128,
                        [[HQL * t_batch, 128], [t_batch, HQL], [1, 128]]),
                bass.AP(pT[1], 0, [[1024, 128], [128, HQL], [1, 128]])), "ac")

        with BLK() as blk:
            for ts in range(NT):
                st = sts2[ts]
                cs = ts % 2

                def p2wrap(v, ts=ts, st=st, cs=cs):
                    tk.wait(v, DVE, tk.tok("dv"))
                    tk.wait(v, DVE, bt2["inv"], st["fe"], sts2[NT - 1]["fm"])
                    if ts >= 2:
                        tk.wait(v, DVE, sts2[ts - 2]["sin"])
                    for poff, nh, coff in [(P0Q_OFF, HQL, C_NQ),
                                           (P0K_OFF, HKV, C_NQ + HQL)]:
                        sap = bass.AP(SBf, poff + ts * 512, [[SBW, 128], [64, nh], [1, 64]])
                        iap = bass.AP(ROWS, ts * 24 + coff, [[160, 128], [1, nh], [0, 64]])
                        ins = v.tensor_tensor(sap, sap, iap, op=ALU.mult)
                    tk.inc(ins, "dv")
                    fq = bass.AP(SBf, P0Q_OFF + ts * 512, [[SBW, 128], [1, 512]])
                    fk = bass.AP(SBf, P0K_OFF + ts * 512, [[SBW, 128], [1, 512]])
                    Cq = QKT[:, cs * 1024: cs * 1024 + 512]
                    Ck = QKT[:, cs * 1024 + 512: cs * 1024 + 1024]
                    t1, t2 = TMP1[:, 0:512], TMP1[:, 512:1024]
                    t3, t4 = TMP2[:, 0:512], TMP2[:, 512:1024]
                    u2 = bass.AP(SBf, QT2_OFF, [[SBW, 128], [1, 512]])
                    u4 = bass.AP(SBf, QT2_OFF + 512, [[SBW, 128], [1, 512]])
                    tk.wait(v, DVE, tk.tok("dv"))
                    v.tensor_scalar_add(Cq, fq, PI / 2)
                    v.tensor_scalar_add(Ck, fk, PI / 2)
                    v.tensor_scalar(t1, fq, -PI, 2 * PI, op0=ALU.is_lt, op1=ALU.mult)
                    v.tensor_scalar(t2, fq, PI, 2 * PI, op0=ALU.is_gt, op1=ALU.mult)
                    v.tensor_scalar(t3, fk, -PI, 2 * PI, op0=ALU.is_lt, op1=ALU.mult)
                    tk.inc(v.tensor_scalar(t4, fk, PI, 2 * PI,
                                           op0=ALU.is_gt, op1=ALU.mult), "dv")
                    tk.wait(v, DVE, tk.tok("dv"))
                    v.tensor_tensor(fq, fq, t1, op=ALU.add)
                    tk.inc(v.tensor_tensor(fk, fk, t3, op=ALU.add), "dv")
                    tk.wait(v, DVE, tk.tok("dv"))
                    v.tensor_tensor(fq, fq, t2, op=ALU.subtract)
                    v.tensor_tensor(fk, fk, t4, op=ALU.subtract)
                    v.tensor_scalar(t1, Cq, -PI, 2 * PI, op0=ALU.is_lt, op1=ALU.mult)
                    v.tensor_scalar(t3, Ck, -PI, 2 * PI, op0=ALU.is_lt, op1=ALU.mult)
                    v.tensor_scalar(u2, Cq, PI, 2 * PI, op0=ALU.is_gt, op1=ALU.mult)
                    tk.inc(v.tensor_scalar(u4, Ck, PI, 2 * PI,
                                           op0=ALU.is_gt, op1=ALU.mult), "dv")
                    tk.wait(v, DVE, tk.tok("dv"))
                    v.tensor_tensor(Cq, Cq, t1, op=ALU.add)
                    tk.inc(v.tensor_tensor(Ck, Ck, t3, op=ALU.add), "dv")
                    tk.wait(v, DVE, tk.tok("dv"))
                    v.tensor_tensor(Cq, Cq, u2, op=ALU.subtract)
                    st["wrap"] = tk.inc(v.tensor_tensor(Ck, Ck, u4, op=ALU.subtract), "dv")
                blk.vector(p2wrap)

                if ts > 0:
                    emit_p3m(blk, ts - 1)

                def p2sin(a, ts=ts, st=st, cs=cs):
                    tk.wait(a, ACT, st["wrap"], tk.tok("dbg"))
                    for nm2, nh in [("q", HQL), ("k", HKV)]:
                        dstb = QP if nm2 == "q" else KP
                        poff = P0Q_OFF if nm2 == "q" else P0K_OFF
                        dwid = nh * DH
                        coff = cs * 1024 + (0 if nm2 == "q" else 512)
                        a.activation(
                            bass.AP(dstb, ts * dwid, [[NT * dwid, 128], [128, nh], [1, 64]]),
                            bass.AP(SBf, poff + ts * 512, [[SBW, 128], [64, nh], [1, 64]]),
                            AF.Sin)
                        ins = a.activation(
                            bass.AP(dstb, ts * dwid + 64,
                                    [[NT * dwid, 128], [128, nh], [1, 64]]),
                            bass.AP(QKT, coff, [[2048, 128], [64, nh], [1, 64]]),
                            AF.Sin)
                    st["sin"] = tk.inc(ins, "ac")
                    prev_sin[0] = st["sin"]
                    sin_slot_prev[ts] = st["sin"]
                    if ts > 0:
                        p4te_ops(a, ts - 1)
                    if nxt is not None:
                        p1a_ops(a, nxt, ts)
                blk.scalar(p2sin)

                def p3a(v, ts=ts, st=st):
                    tk.wait(v, DVE, tk.tok("dv"))
                    tk.wait(v, DVE, tk.tok("dbg"), pool_prev[0])
                    tk.wait(v, DVE, st["sin"])
                    for h in range(HQL):
                        if h > 0:
                            tk.wait(v, DVE, tk.tok("dv"))
                        for j in range(HKV):
                            qsl = bass.AP(QP, ts * HQL * DH + h * 128,
                                          [[NT * HQL * DH, 128], [1, 128]])
                            ksl = bass.AP(KP, ts * HKV * DH + j * 128,
                                          [[NT * HKV * DH, 128], [1, 128]])
                            ins = v.scalar_tensor_tensor(
                                ATMP[:, j * 128:(j + 1) * 128], qsl, 1.0 / 64.0, ksl,
                                op0=ALU.mult, op1=ALU.mult,
                                accum_out=ROWS[:, 96 + h * 8 + j:97 + h * 8 + j])
                        tk.inc(ins, "dv")
                    a_done[ts] = tk.tok("dv")
                blk.vector(p3a)

                if nxt is not None:
                    emit_p1t(blk, nxt, ts + 1)
                elif ts < 2:
                    emit_p4g(blk, ts)

                def p3p(g, ts=ts, st=st):
                    tk.wait(g, "pl", a_done[ts])
                    tk.wait(g, "pl", evt.get((4, ts)), evt.get((5, ts)))
                    vfull = bass.AP(VB, ts * HKV * DH, [[NT * HKV * DH, 128], [1, 1024]])
                    for h in range(HSPL, HQL):
                        abb = bass.AP(ROWS, 96 + h * 8, [[160, 128], [1, 8], [0, 128]])
                        tk.wait(g, "pl", tk.tok("po"))
                        tk.inc(g.tensor_tensor(PSCR[:, :], vfull, abb, op=ALU.mult), "po")
                        tk.wait(g, "pl", tk.tok("po"))
                        tk.inc(g.tensor_tensor(PSCR[:, 0:512], PSCR[:, 0:512],
                                               PSCR[:, 512:1024], op=ALU.add), "po")
                        tk.wait(g, "pl", tk.tok("po"))
                        tk.inc(g.tensor_tensor(PSCR[:, 0:256], PSCR[:, 0:256],
                                               PSCR[:, 256:512], op=ALU.add), "po")
                        msl = bass.AP(SBf, MIX_OFF + ts * HQL * DH + h * 128,
                                      [[SBW, 128], [1, 128]])
                        tk.wait(g, "pl", tk.tok("po"))
                        tk.inc(g.tensor_tensor(msl, PSCR[:, 0:128],
                                               PSCR[:, 128:256], op=ALU.add), "po")
                blk.gpsimd(p3p)
                pool_prev[0] = tk.tok("po")
                pool_mix_tok[ts] = pool_prev[0]
                if debug and ac == 0 and ts == 0:
                    def p3d(g):
                        tk.wait(g, "pl", tk.tok("dv"), tk.tok("po"))
                        tk.inc(g.dma_start(dbg["DBG_A"].ap(), ROWS[:, 96:160]), "dbg", 16)
                    blk.gpsimd(p3d)

            emit_p3m(blk, NT - 1)

            def p4te3(a):
                p4te_ops(a, NT - 1)
            blk.scalar(p4te3)
        pT_free[1] = p4_free[0]

        if ac + 1 < n_ac:
            with BLK() as blk:
                emit_p1_tail(blk, ac + 1, mix_toks)

        if debug and ac == 0:
            with BLK() as blk:
                def pdbg(g):
                    tk.wait(g, "pl", tk.tok("dv"), tk.tok("ac"))
                    for nm2, srcb in [("DBG_QP", QP), ("DBG_KP", KP)]:
                        tk.inc(g.dma_start(dbg[nm2].ap(), srcb[:, :]), "dbg", 16)
                blk.gpsimd(pdbg)

    if debug:
        with BLK() as blk:
            def pdbg2(g):
                tk.wait(g, "pl", tk.tok("dv"), tk.tok("ac"))
                tk.inc(g.dma_start(dbg["DBG_MIXT"].ap(), MIXT[:, :]), "dbg", 16)
            blk.gpsimd(pdbg2)

    pctx.close()
    actx0.close()
    p4ctx = contextlib.ExitStack()
    p4wbuf[0] = p4ctx.enter_context(nc.sbuf_tensor("WOA", [128, 8192], BF16))
    p4wbuf[1] = p4ctx.enter_context(nc.sbuf_tensor("WOB", [128, 8192], BF16))

    attn_done = [tk.tok("dv"), tk.tok("ac"), tk.tok("po")]
    with BLK() as blk:
        bank_tok = {}
        for dc in range(4):
            if dc < 2:
                buf, sem = (WA, "dwa") if dc == 0 else (WB, "dwb")
            else:
                buf, sem = p4wbuf[dc - 2], ("dwc" if dc == 2 else "dwd")
            if dc >= 2:
                emit_p4g(blk, dc, extra=tuple(attn_done))

            for r in range(NSUB):
                def p4t(t, dc=dc, r=r, buf=buf):
                    tk.wait(t, "pe", p4evt[("w", dc)], bank_tok.get(r % 4), *attn_done)
                    bank = P[r % 4]
                    t.matmul(bank[:, :], ONES[:, :], BOs[:, dc * 512:(dc + 1) * 512],
                             start=True, stop=False)
                    last = None
                    for m in range(16):
                        lhsT = bass.AP(MIXT, r * t_batch + m,
                                       [[HQL * t_batch, 128], [16, 128]])
                        last = t.matmul(bank[:, :], lhsT,
                                        buf[:, m * 512:(m + 1) * 512],
                                        start=False, stop=(m == 15))
                    p4evt[("mm", dc, r)] = tk.inc(last, "pe")
                blk.tensor(p4t)

                if r % 2 == 0:
                    def p4v(v, dc=dc, r=r):
                        tk.wait(v, DVE, p4evt[("mm", dc, r)])
                        tk.wait(v, DVE, tk.tok("dv"))
                        bank_tok[r % 4] = tk.inc(v.tensor_copy(
                            bass.AP(SBf, r * DM + dc * 512, [[NSUB * DM, 128], [1, 512]]),
                            P[r % 4][:, :]), "dv")
                    blk.vector(p4v)
                else:
                    def p4a(a, dc=dc, r=r):
                        tk.wait(a, ACT, p4evt[("mm", dc, r)])
                        tk.wait(a, ACT, tk.tok("ac"))
                        bank_tok[r % 4] = tk.inc(a.copy(
                            bass.AP(SBf, r * DM + dc * 512, [[NSUB * DM, 128], [1, 512]]),
                            P[r % 4][:, :]), "ac")
                    blk.scalar(p4a)

            def p4w(t, dc=dc, sem=sem):
                pass
            wcons[sem] = tk.tok("pe")

    p4b_done = [tk.tok("pe"), tk.tok("dv"), tk.tok("ac")]
    p4ctx.close()
    mixt_ctx.close()

    if debug:
        with BLK() as blk:
            def pdbg3(g):
                tk.wait(g, "pl", *p4b_done)
                tk.inc(g.dma_start(dbg["DBG_S"].ap(), SBf[:, :]), "dbg", 16)
            blk.gpsimd(pdbg3)

    fctx0 = contextlib.ExitStack()
    fsb0 = lambda nm, shp, dt: fctx0.enter_context(nc.sbuf_tensor(nm, shp, dt))
    U = fsb0("U", [128, NA * t_chunk], BF16)
    GT8 = fsb0("GT8", [128, NT * 2 * 512], BF16)
    TMPR = fsb0("TMPR", [128, 512], BF16)
    xb_cons_dv[0] = xb_cons_dv[1] = None
    xb_cons_ac[0] = xb_cons_ac[1] = None
    ffn_prev = []
    p5_war = list(p4b_done)
    s1_free = [()]
    s2_free = [()]

    for oc2 in range(n_oc):
        orow0 = oc2 * t_chunk

        p0ctx = contextlib.ExitStack()
        bufs["XBa"] = p0ctx.enter_context(nc.sbuf_tensor(f"XBaf_{oc2}", [128, DM], F32))
        bufs["XBb"] = p0ctx.enter_context(nc.sbuf_tensor(f"XBbf_{oc2}", [128, DM], F32))
        bufs["XNa"] = p0ctx.enter_context(nc.sbuf_tensor(f"XNaf_{oc2}", [128, DM], BF16))
        bufs["XNb"] = p0ctx.enter_context(nc.sbuf_tensor(f"XNbf_{oc2}", [128, DM], BF16))
        pctx = contextlib.ExitStack()
        pT = [pctx.enter_context(nc.psum_tensor(f"pTf{i}_{oc2}", [128, 1024], BF16))
              for i in range(2)]
        pT_free = {0: None, 1: None}
        with BLK() as blk:
            emit_ln_chunk(blk, XOWN, orow0, pT, pT_free,
                          extra_war=tuple(p5_war))
        pctx.close()
        p0ctx.close()

        f5ctx = contextlib.ExitStack()
        W3A = f5ctx.enter_context(nc.sbuf_tensor(f"W3A_{oc2}", [128, 4096], BF16))
        W3B = f5ctx.enter_context(nc.sbuf_tensor(f"W3B_{oc2}", [128, 4096], BF16))
        GT = f5ctx.enter_context(nc.sbuf_tensor(f"GT_{oc2}", [128, 1024], BF16))

        h128_inline = not (debug and oc2 == 0)
        fw = {}
        hop1_sems = []

        def emit_h128(blk, a_):
            def p6m(t, a_=a_):
                bank = P[4 + a_ % 2]
                tk.wait(t, "pe", upt[("gate", a_)], fw.get(("e128", a_ - 2)))
                ins = t.matmul(bank[:, :t_chunk], H128S[:, :],
                               U[:, a_ * t_chunk:(a_ + 1) * t_chunk],
                               start=True, stop=True)
                fw[("m128", a_)] = tk.inc(ins, "pe")
            blk.tensor(p6m)

            def p6e(v, a_=a_):
                bank = P[4 + a_ % 2]
                tk.wait(v, DVE, tk.tok("dbg"), fw[("m128", a_)])
                tk.wait(v, DVE, tk.tok("dv"))
                fw[("e128", a_)] = tk.inc(v.tensor_copy(
                    U[:, a_ * t_chunk:(a_ + 1) * t_chunk], bank[:, :t_chunk]), "dv")
            blk.vector(p6e)

        def emit_hop1_piece(blk, g):
            a0 = g * 8
            if g % 2 == 0:
                def p6h1s(sp, a0=a0):
                    tk.wait(sp, "sp", fw[("e128", a0 + 7)], *s1_free[0])
                    tk.inc(sp.dma_start(
                        bass.AP(S1, a0 * t_chunk,
                                [[64 * t_chunk, 128], [t_chunk, 8], [1, t_chunk]]),
                        bass.AP(U, a0 * t_chunk,
                                [[NA * t_chunk, 128], [t_chunk, 8], [1, t_chunk]]),
                    ), "dh", 16)
                blk.sync(p6h1s)
                hop1_sems.append(tk.tok("dh"))
            else:
                def p6h1a(a, a0=a0):
                    tk.wait(a, ACT, fw[("e128", a0 + 7)], *s1_free[0])
                    tk.inc(a.dma_start(
                        bass.AP(S1, a0 * t_chunk,
                                [[64 * t_chunk, 128], [t_chunk, 8], [1, t_chunk]]),
                        bass.AP(U, a0 * t_chunk,
                                [[NA * t_chunk, 128], [t_chunk, 8], [1, t_chunk]]),
                    ), "dh3", 16)
                blk.scalar(p6h1a)
                hop1_sems.append(tk.tok("dh3"))

        with BLK() as blk:
            upt = {}
            for gb in range(32):
                b1, s1_ = (WA, "dwa") if gb % 2 == 0 else (WB, "dwb")
                b3, s3_ = (W3A, "dwc") if gb % 2 == 0 else (W3B, "dwd")

                def p5g(g, gb=gb, b1=b1, s1_=s1_, b3=b3, s3_=s3_):
                    tk.wait(g, "pl", tk.tok("dbg"), *out_slot_tok, *xres_slot_tok)
                    tk.wait(g, "pl", xb_cons_dv[0], xb_cons_dv[1],
                            xb_cons_ac[0], xb_cons_ac[1])
                    tk.wait(g, "pl", wcons[s1_], tk.tok("dwe"))
                    tk.inc(g.dma_start(
                        bass.AP(b1, 0, [[8192, 128], [1, 4096]]),
                        bass.AP(W1Bd, gb * 128 * 4096, [[4096, 128], [1, 4096]]),
                    ), s1_, 16)
                    upt[("w1", gb)] = tk.tok(s1_)
                    tk.wait(g, "pl", wcons[s3_])
                    tk.inc(g.dma_start(
                        bass.AP(b3, 0, [[4096, 128], [1, 4096]]),
                        bass.AP(W3Bd, gb * 128 * 4096, [[4096, 128], [1, 4096]]),
                    ), s3_, 16)
                    upt[("w3", gb)] = tk.tok(s3_)
                blk.gpsimd(p5g)

                for al in range(2):
                    a_ = gb * 2 + al
                    pu, pl_ = (P[0], P[1]) if a_ % 2 == 0 else (P[2], P[3])

                    def p5t(t, gb=gb, al=al, a_=a_, b1=b1, b3=b3, pu=pu, pl_=pl_):
                        tk.wait(t, "pe", upt[("w1", gb)], upt[("w3", gb)],
                                pT_free[0], pT_free[1], *p5_war)
                        if ("gate", a_ - 2) in upt:
                            tk.wait(t, "pe", upt[("gate", a_ - 2)])
                        last = None
                        for k in range(NDT):
                            last = t.matmul(pu[:, :t_chunk],
                                            b1[:, k * 256 + al * 128: k * 256 + (al + 1) * 128],
                                            XNT[:, k * t_chunk:(k + 1) * t_chunk],
                                            start=(k == 0), stop=(k == NDT - 1))
                        for k in range(NDT):
                            last = t.matmul(pl_[:, :t_chunk],
                                            b3[:, k * 256 + al * 128: k * 256 + (al + 1) * 128],
                                            XNT[:, k * t_chunk:(k + 1) * t_chunk],
                                            start=(k == 0), stop=(k == NDT - 1))
                        upt[("mm", a_)] = tk.inc(last, "pe")
                        if al == 1:
                            wcons["dwa" if gb % 2 == 0 else "dwb"] = tk.tok("pe")
                            wcons["dwc" if gb % 2 == 0 else "dwd"] = tk.tok("pe")
                    blk.tensor(p5t)

                    def p5a(a, a_=a_, pu=pu):
                        tk.wait(a, ACT, upt[("mm", a_)])
                        tk.wait(a, ACT, tk.tok("ac"))
                        upt[("silu", a_)] = tk.inc(a.activation(
                            GT[:, (a_ % 2) * 512:(a_ % 2) * 512 + t_chunk],
                            pu[:, :t_chunk], AF.Silu), "ac")
                    blk.scalar(p5a)

                    def p5v(v, a_=a_, pl_=pl_):
                        tk.wait(v, DVE, upt[("silu", a_)])
                        tk.wait(v, DVE, tk.tok("dv"))
                        upt[("gate", a_)] = tk.inc(v.tensor_tensor(
                            U[:, a_ * t_chunk:(a_ + 1) * t_chunk],
                            GT[:, (a_ % 2) * 512:(a_ % 2) * 512 + t_chunk],
                            pl_[:, :t_chunk], op=ALU.mult), "dv")
                    blk.vector(p5v)

                if h128_inline and gb > 0:
                    for a_ in (2 * gb - 2, 2 * gb - 1):
                        emit_h128(blk, a_)
                    if gb % 4 == 0:
                        emit_hop1_piece(blk, gb // 4 - 1)
            if h128_inline:
                for a_ in (NA - 2, NA - 1):
                    emit_h128(blk, a_)
                emit_hop1_piece(blk, 7)

        xnt_cons[0] = tk.tok("pe")

        if debug and oc2 == 0:
            with BLK() as blk:
                def pdbgu(g):
                    tk.wait(g, "pl", upt[("gate", NA - 1)])
                    tk.inc(g.dma_start(dbg["DBG_U"].ap(), U[:, :]), "dbg", 16)
                blk.gpsimd(pdbgu)

        with BLK() as blk:
            if not h128_inline:
                for a_ in range(NA):
                    emit_h128(blk, a_)
                for g in range(8):
                    emit_hop1_piece(blk, g)
            h1 = (tk.tok("dh"), tk.tok("dh3"))

            def h2piece(th, ch):
                dst = bass.AP(U, th * 64 * (NA * t_chunk) + ch * 64 * TH,
                              [[NA * t_chunk, 64], [TH, 64], [1, TH]])
                srcp = bass.AP(S1, th * TH + ch * 64 * (64 * t_chunk),
                               [[t_chunk, 64], [64 * t_chunk, 64], [1, TH]])
                return dst, srcp

            def p6h2s(sp):
                tk.wait(sp, "sp", *h1)
                for (th, ch), sem in (((0, 0), "dh"), ((1, 0), "dh4")):
                    dst, srcp = h2piece(th, ch)
                    tk.inc(sp.dma_start(dst, srcp), sem, 16)
                fw["hop2a"] = (tk.tok("dh"), tk.tok("dh4"))
            blk.sync(p6h2s)

            def p6h2a(a):
                tk.wait(a, ACT, *h1)
                for (th, ch), sem in (((0, 1), "dh3"), ((1, 1), "dh5")):
                    dst, srcp = h2piece(th, ch)
                    tk.inc(a.dma_start(dst, srcp), sem, 16)
                fw["hop2b"] = (tk.tok("dh3"), tk.tok("dh5"))
            blk.scalar(p6h2a)
            s1_free[0] = fw["hop2a"] + fw["hop2b"]

            nh64 = (128 * TH) // 512
            for ch in range(nh64):
                def p6m2(t, ch=ch):
                    bank = P[ch % 4]
                    tk.wait(t, "pe", *(fw["hop2a"] if ch < nh64 // 2 else fw["hop2b"]))
                    if ("e64", ch - 4) in fw:
                        tk.wait(t, "pe", fw[("e64", ch - 4)])
                    ins = t.matmul(bank[:, :], BLK64[:, :], U[:, ch * 512:(ch + 1) * 512],
                                   start=True, stop=True)
                    fw[("m64", ch)] = tk.inc(ins, "pe")
                blk.tensor(p6m2)

                def p6e2(v, ch=ch):
                    bank = P[ch % 4]
                    tk.wait(v, DVE, fw[("m64", ch)])
                    tk.wait(v, DVE, tk.tok("dv"))
                    fw[("e64", ch)] = tk.inc(v.tensor_copy(
                        U[:, ch * 512:(ch + 1) * 512], bank[:, :]), "dv")
                blk.vector(p6e2)

            def h3piece(th, ch):
                dst = bass.AP(S2, th * TH + ch * 64 * t_chunk,
                              [[128 * t_chunk, 64], [t_chunk, 64], [1, TH]])
                srcp = bass.AP(U, th * 64 * (NA * t_chunk) + ch * 64 * TH,
                               [[NA * t_chunk, 64], [TH, 64], [1, TH]])
                return dst, srcp

            def p6h3s(sp):
                tk.wait(sp, "sp", fw[("e64", nh64 // 2 - 1)], *s2_free[0])
                for th, ch in ((0, 0), (1, 0)):
                    dst, srcp = h3piece(th, ch)
                    tk.inc(sp.dma_start(dst, srcp), "dh", 16)
            blk.sync(p6h3s)

            def p6h3a(a):
                tk.wait(a, ACT, fw[("e64", nh64 - 1)], *s2_free[0])
                for th, ch in ((0, 1), (1, 1)):
                    dst, srcp = h3piece(th, ch)
                    tk.inc(a.dma_start(dst, srcp), "dh3", 16)
            blk.scalar(p6h3a)
            h3 = (tk.tok("dh"), tk.tok("dh3"))

            def h4piece(gg):
                dst = bass.AP(U, gg * 8 * t_chunk,
                              [[NA * t_chunk, 128], [t_chunk, 8], [1, t_chunk]])
                srcp = bass.AP(S2, gg * 8 * 128 * t_chunk,
                               [[t_chunk, 128], [128 * t_chunk, 8], [1, t_chunk]])
                return dst, srcp

            hop4_p = [None] * 8

            def p6h4s(sp):
                tk.wait(sp, "sp", *h3)
                for gg in range(0, 8, 2):
                    dst, srcp = h4piece(gg)
                    tk.inc(sp.dma_start(dst, srcp), "dh", 16)
                    hop4_p[gg] = tk.tok("dh")
            blk.sync(p6h4s)

            def p6h4a(a):
                tk.wait(a, ACT, *h3)
                for gg in range(1, 8, 2):
                    dst, srcp = h4piece(gg)
                    tk.inc(a.dma_start(dst, srcp), "dh3", 16)
                    hop4_p[gg] = tk.tok("dh3")
            blk.scalar(p6h4a)
            fw["hop4"] = (hop4_p[6], hop4_p[7])
            s2_free[0] = fw["hop4"]

        if debug and oc2 == 0:
            with BLK() as blk:
                def p6d(g):
                    tk.wait(g, "pl", *fw["hop4"])
                    tk.inc(g.dma_start(dbg["DBG_FB"].ap(), U[:, :]), "dbg", 16)
                blk.gpsimd(p6d)

        f5ctx.close()
        f7ctx = contextlib.ExitStack()
        XRES = [f7ctx.enter_context(nc.sbuf_tensor(f"XRES{i}_{oc2}", [128, 1024], F32))
                for i in range(NT)]
        OUTB = [f7ctx.enter_context(nc.sbuf_tensor(f"OUTB{i}_{oc2}", [128, 1024], F32))
                for i in range(NT)]
        p67 = contextlib.ExitStack()
        P6b = p67.enter_context(nc.psum_tensor(f"p6_{oc2}", [128, 512], F32))
        P7b = p67.enter_context(nc.psum_tensor(f"p7_{oc2}", [128, 512], F32))
        BANKS8 = list(P[:6]) + [P6b, P7b]

        with BLK() as blk:
            dn = {}
            hop4_tok = fw["hop4"]

            for dh_ in range(2):
                def p7x(sp, dh_=dh_):
                    tk.wait(sp, "sp", tk.tok("dbg"))
                    for ts in range(NT):
                        sem = f"xr{ts}"
                        tk.wait(sp, "sp", xres_slot_tok[ts])
                        tk.inc(sp.dma_start(
                            XRES[ts][:, :],
                            bass.AP(XOWN, (orow0 + ts * 128) * DM + dh_ * 1024,
                                    [[DM, 128], [1, 1024]]),
                        ), sem, 16)
                        dn[("xr", dh_, ts)] = tk.tok(sem)
                blk.sync(p7x)

                for gb in range(16):
                    buf, sem = (WA, "dwa") if gb % 2 == 0 else (WB, "dwb")

                    def p7g(g, dh_=dh_, gb=gb, buf=buf, sem=sem):
                        tk.wait(g, "pl", wcons[sem], tk.tok("dwe"))
                        tk.inc(g.dma_start(
                            bass.AP(buf, 0, [[8192, 128], [1, 4096]]),
                            bass.AP(W2Bd, (dh_ * 16 + gb) * 128 * 4096,
                                    [[4096, 128], [1, 4096]]),
                        ), sem, 16)
                        dn[("w", dh_, gb)] = tk.tok(sem)
                    blk.gpsimd(p7g)

                    def p7t(t, dh_=dh_, gb=gb, buf=buf, sem=sem):
                        tk.wait(t, "pe", *fw["hop4"])
                        tk.wait(t, "pe", dn[("w", dh_, gb)])
                        if gb == 0 and dh_ == 1:
                            tk.wait(t, "pe", dn[("silu", 0)])
                        last = None
                        for al in range(4):
                            a_ = gb * 4 + al
                            for ts in range(NT):
                                for dc in range(2):
                                    bank = BANKS8[(ts * 2 + dc) % len(BANKS8)]
                                    last = t.matmul(
                                        bank[:, :],
                                        U[:, a_ * t_chunk + ts * 128: a_ * t_chunk + (ts + 1) * 128],
                                        buf[:, al * 1024 + dc * 512: al * 1024 + (dc + 1) * 512],
                                        start=(a_ == 0), stop=(a_ == NA - 1))
                        gtok = tk.inc(last, "pe")
                        if gb == 15:
                            dn[("mm", dh_)] = gtok
                        wcons[sem] = gtok
                    blk.tensor(p7t)

                def p7s(a, dh_=dh_):
                    tk.wait(a, ACT, dn[("mm", dh_)], dn.get(("vdone", dh_ - 1)))
                    for i in range(NT * 2):
                        tk.wait(a, ACT, tk.tok("ac"))
                        tk.inc(a.activation(GT8[:, i * 512:(i + 1) * 512],
                                            BANKS8[i % len(BANKS8)][:, :], AF.Silu), "ac")
                    dn[("silu", dh_)] = tk.tok("ac")
                blk.scalar(p7s)

                def p7v(v, dh_=dh_, oc2=oc2):
                    tk.wait(v, DVE, dn[("silu", dh_)])
                    for ts in range(NT):
                        ob, xr = OUTB[ts], XRES[ts]
                        tk.wait(v, DVE, dn[("xr", dh_, ts)])
                        tk.wait(v, DVE, out_slot_tok[ts])
                        for dc in range(2):
                            i = ts * 2 + dc
                            r = oc2 * NT + ts
                            ssl = bass.AP(SBf, r * DM + dh_ * 1024 + dc * 512,
                                          [[NSUB * DM, 128], [1, 512]])
                            tk.wait(v, DVE, tk.tok("dv"))
                            tk.inc(v.tensor_tensor(TMPR[:, :], ssl,
                                                   GT8[:, i * 512:(i + 1) * 512],
                                                   op=ALU.mult), "dv")
                            tk.wait(v, DVE, tk.tok("dv"))
                            tk.inc(v.tensor_tensor(ob[:, dc * 512:(dc + 1) * 512],
                                                   xr[:, dc * 512:(dc + 1) * 512],
                                                   TMPR[:, :], op=ALU.add), "dv")
                        dn[("out", dh_, ts)] = tk.tok("dv")
                        xres_slot_tok[ts] = tk.tok("dv")
                    dn[("vdone", dh_)] = tk.tok("dv")
                blk.vector(p7v)

                def p7o(sp, dh_=dh_):
                    for ts in range(NT):
                        sem = f"do{ts}"
                        tk.wait(sp, "sp", dn[("out", dh_, ts)])
                        tk.inc(sp.dma_start(
                            bass.AP(OUT, (orow0 + ts * 128) * DM + dh_ * 1024,
                                    [[DM, 128], [1, 1024]]),
                            OUTB[ts][:, :]), sem, 16)
                        out_slot_tok[ts] = tk.tok(sem)
                blk.sync(p7o)

        ffn_prev = [tk.tok("dv"), tk.tok("ac")]
        p5_war = [dn[("silu", 1)]]
        p67.close()
        f7ctx.close()

    fctx0.close()

    with BLK() as blk:
        def fin(sp):
            tk.wait(sp, "sp", tk.tok("do0"), tk.tok("do1"), tk.tok("do2"), tk.tok("do3"),
                    tk.tok("dh"), tk.tok("dh3"), tk.tok("dh4"), tk.tok("dh5"),
                    tk.tok("dwe"))
            if debug:
                tk.wait(sp, "sp", tk.tok("dbg"))
        blk.sync(fin)

    _mega.__exit__(None, None, None)
    ctx.close()
    nc.finalize()
    return nc



def _tile_dT(wt, ncols):
    d, o = wt.shape
    a = wt.reshape(NDT, 128, o // ncols, ncols)
    a = a.transpose(2, 1, 0, 3).reshape(o // ncols, 128, NDT * ncols)
    return np.ascontiguousarray(a)


def _tile_w2(w2t):
    a = w2t.reshape(16, 4, 128, 2, 1024)
    a = a.transpose(3, 0, 2, 1, 4).reshape(32, 128, 4096)
    return np.ascontiguousarray(a)


def _host_common(inputs):
    f32 = lambda a: np.ascontiguousarray(np.asarray(a, np.float32))
    return {
        "WKT": _tile_dT(np.asarray(inputs["L0_attn_Wk"], np.float32).T, 512),
        "WVT": _tile_dT(np.asarray(inputs["L0_attn_Wv"], np.float32).T, 512),
        "WOT": _tile_dT(np.asarray(inputs["L0_attn_Wo"], np.float32).T, 512),
        "W1T": _tile_dT(np.asarray(inputs["L0_ffn_W1"], np.float32).T, 256),
        "W3T": _tile_dT(np.asarray(inputs["L0_ffn_W3"], np.float32).T, 256),
        "W2T": _tile_w2(np.asarray(inputs["L0_ffn_W2"], np.float32).T),
        "BK": f32(inputs["L0_attn_bk"]).reshape(1, -1),
        "BV": f32(inputs["L0_attn_bv"]).reshape(1, -1),
        "BO": f32(inputs["L0_attn_bo"]).reshape(1, -1),
        "RT": f32(np.asarray(inputs["L0_attn_rand_matrix"], np.float32).T),
        "IDENT": np.eye(128, dtype=np.float32),
        "ONESR": np.ones((1, 128), np.float32),
        "H128S": (_hadamard(128) / 16.0).astype(np.float32),
        "BLK64": np.kron(np.eye(2), _hadamard(64) * (np.sqrt(2.0) / 8.0)).astype(np.float32),
    }


def _rope_tables(positions):
    inv_freq = 1.0 / (10000.0 ** (np.arange(0, DH, 2, dtype=np.float64) / DH))
    freqs = positions[:, None].astype(np.float64) * inv_freq[None, :]
    cos = np.cos(freqs).astype(np.float32)
    sin = np.sin(freqs).astype(np.float32)
    cos_full = np.ascontiguousarray(np.concatenate([cos, cos], axis=1))
    sin_signed = np.ascontiguousarray(np.concatenate([-sin, sin], axis=1))
    return cos_full, sin_signed


def kernel(**inputs):
    X = np.ascontiguousarray(np.asarray(inputs["L0_attn_hidden_states"], np.float32)
                             ).reshape(8192, DM)
    common = _host_common(inputs)
    WqT = np.ascontiguousarray(np.asarray(inputs["L0_attn_Wq"], np.float32).T)
    bq = np.asarray(inputs["L0_attn_bq"], np.float32)
    cosf, sins = _rope_tables(np.arange(TBATCH))
    common["COS"] = cosf
    common["SINS"] = sins
    nc = build_program(debug=False)
    in_maps = []
    for core in range(NCORES):
        b, par = core // 2, core % 2
        m = dict(common)
        hbase = par * HQL * DH
        m["WQTS"] = _tile_dT(WqT[:, hbase:hbase + HQL * DH], 512)
        m["BQ"] = np.ascontiguousarray(bq[hbase:hbase + HQL * DH]).reshape(1, -1)
        m["XBAT"] = np.ascontiguousarray(X[b * TBATCH:(b + 1) * TBATCH])
        m["XOWN"] = np.ascontiguousarray(
            X[b * TBATCH + par * TOWN: b * TBATCH + (par + 1) * TOWN])
        in_maps.append(m)
    res = run_bass_kernel_spmd(nc, in_maps, core_ids=list(range(NCORES)))
    out = np.concatenate([np.asarray(r["OUT"], np.float32) for r in res.results], axis=0)
    return out.reshape(4, 2048, DM)
